# version 01 (final)
# baseline (speedup 1.0000x reference)
import contextlib
import numpy as np
import ml_dtypes
import concourse.bass as bass
import concourse.bacc as bacc
import concourse.mybir as mybir
from concourse.bass_utils import run_bass_kernel_spmd

F32 = mybir.dt.float32
BF16 = mybir.dt.bfloat16
U16 = mybir.dt.uint16
AF = mybir.ActivationFunctionType
ALU = mybir.AluOpType
AX = mybir.AxisListType
ts = bass.ts

T = 4096
D = 2048
FT = D // 128
TT = T // 128
QB = T // 512

C_EPS5 = 0
C_NHALF = 1
C_INVD = 2
C_EPS8 = 3
C_QSCALE = 5


class P:
    def __init__(self, nc):
        self.nc = nc
        self._sems = {}
        self._fori_regs = {}
        self._fori_depth = 0

    def sem(self, name):
        if name not in self._sems:
            self._sems[name] = self.nc.alloc_semaphore(name)
        return self._sems[name]

    @contextlib.contextmanager
    def fori(self, start, end):
        nc = self.nc
        d = self._fori_depth
        self._fori_depth += 1
        regs = self._fori_regs.get(d)
        if regs is None:
            regs = nc.alloc_registers(f"myfori_d{d}")
            self._fori_regs[d] = regs
        loop_id = nc.next_id()
        loop_start = f"myfori_{loop_id}_loop"
        loop_end = f"myfori_{loop_id}_end"
        nc.regs_mov(regs, start)
        nc.br(loop_start)
        with nc.body(loop_start):
            yield nc.snap(regs, min_val=int(start), max_val=int(end) - 1)
            nc.regs_alu(regs, regs, 1, op=mybir.AluOpType.add)
            nc.br_lt(regs, end, on_true=loop_start, on_false=loop_end)
        nc.switch_bb(loop_end)
        self._fori_depth -= 1



def _strips(wT):
    I, O = wT.shape
    return np.ascontiguousarray(
        wT.reshape(I // 128, 128, O // 128, 128).transpose(2, 0, 1, 3))



def emit_consts(p, cst, onecol, onerow, identb, ident_in):
    nc = p.nc
    s = p.sem("CST")
    sd = p.sem("CSTD")
    with nc.Block() as blk:
        @blk.vector
        def _(v):
            v.memset(cst[:, 0:1], 1e-5)
            v.memset(cst[:, 1:2], -0.5)
            v.memset(cst[:, 2:3], 1.0 / D)
            v.memset(cst[:, 3:4], 1e-8)
            v.memset(cst[:, 4:5], 0.0)
            v.memset(cst[:, 5:6], float(1.0 / np.sqrt(128.0)))
            v.memset(cst[:, 6:7], 1.0 / 512.0)
            v.memset(cst[:, 7:8], 10240.0)
            v.memset(onecol[:], 1.0)
            v.memset(onerow[:], 1.0)
            v.memset(p.onerow_f[:], 1.0).then_inc(s, 1)

        @blk.gpsimd
        def _(g):
            g.dma_start(out=identb[:], in_=ident_in[:]).then_inc(sd, 16)
    return s



def emit_stage_norm(p, tag, x_dram, xbf, cst, identb, onecol, onerow, gate=None):
    nc = p.nc
    sX = p.sem(tag + "X")
    sXR = p.sem(tag + "XR")
    sQ = p.sem(tag + "Q")
    sM = p.sem(tag + "M")
    sE = p.sem(tag + "E")
    sB = p.sem(tag + "B")
    sN = p.sem(tag + "N")

    with contextlib.ExitStack() as es:
        E = es.enter_context
        ps_st = E(nc.psum_tensor(tag + "_psst", [128, 64], F32))
        sq2 = E(nc.sbuf_tensor(tag + "_sb_sq2", [128, 2 * T], BF16))
        with nc.Block() as b0:
            @b0.tensor
            def _(te):
                te.nop().then_inc(sM, 1)

        for f in range(FT):
            par = f % 2
            with nc.Block() as blk:
                @blk.gpsimd
                def _(g, f=f):
                    if gate is not None and f == 0:
                        g.wait_ge(gate[0], gate[1])
                    g.dma_start(out=xbf[:, ts(f, T)],
                                in_=x_dram[ts(f, 128), :]).then_inc(sX, 16)

                @blk.scalar
                def _(sc, f=f, par=par):
                    sc.wait_ge(sX, (f + 1) * 16)
                    sc.wait_ge(sM, f)
                    sc.activation(sq2[:, par * T:(par + 1) * T],
                                  xbf[:, ts(f, T)], AF.Square).then_inc(sQ, 1)

                @blk.tensor
                def _(te, f=f, par=par):
                    te.wait_ge(sQ, f + 1)
                    if f == 0:
                        te.wait_ge(p.sem("CST"), 1)
                        te.wait_ge(p.sem("CSTD"), 16)
                    last = None
                    for tt in range(TT):
                        te.matmul(ps_st[:, tt:tt + 1],
                                  xbf[:, ts(f, T)][:, ts(tt, 128)], onecol[:],
                                  start=(f == 0 and tt == 0),
                                  stop=(f == FT - 1 and tt == TT - 1) and False)
                        last = te.matmul(ps_st[:, 32 + tt:33 + tt],
                                         sq2[:, par * T:(par + 1) * T][:, ts(tt, 128)],
                                         onecol[:],
                                         start=False,
                                         stop=(f == FT - 1 and tt == TT - 1))
                    last.then_inc(sM, 1)

        stats = E(nc.sbuf_tensor(tag + "_sb_stats", [128, 64], F32))
        m2 = E(nc.sbuf_tensor(tag + "_sb_m2", [128, 32], F32))
        mrp = E(nc.sbuf_tensor(tag + "_sb_mrp", [128, 64], BF16))
        trp = E(nc.psum_tensor(tag + "_trp", [64, 128], BF16))
        trs = E(nc.sbuf_tensor(tag + "_sb_trs", [64, 128], BF16))
        rowdr = nc.dram_tensor(tag + "_rowdr", [64, 128], BF16, kind="Internal").ap()
        mrow = E(nc.sbuf_tensor(tag + "_sb_mrow", [1, 2 * T], BF16))
        ps_bc = E(nc.psum_tensor(tag + "_psbc", [128, 1024], F32))
        mbc = E(nc.sbuf_tensor(tag + "_sb_mbc", [128, 2 * T], BF16))

        with nc.Block() as blk:
            @blk.vector
            def _(v):
                v.wait_ge(sM, FT + 1)
                v.tensor_copy(stats[:], ps_st[:])
                v.drain()
                v.tensor_scalar(stats[:, 0:32], stats[:, 0:32], cst[:, 2:3], None, ALU.mult)
                v.tensor_scalar(stats[:, 32:64], stats[:, 32:64], cst[:, 2:3], None, ALU.mult)
                v.drain()
                v.tensor_tensor(m2[:], stats[:, 0:32], stats[:, 0:32], ALU.mult)
                v.drain()
                v.tensor_tensor(stats[:, 32:64], stats[:, 32:64], m2[:],
                                ALU.subtract).then_inc(sE, 1)

            @blk.scalar
            def _(sc):
                sc.wait_ge(sE, 1)
                sc.activation(stats[:, 32:64], stats[:, 32:64], AF.Ln, bias=cst[:, 0:1])
                sc.activation(stats[:, 32:64], stats[:, 32:64], AF.Exp,
                              scale=cst[:, 1:2]).then_inc(sE, 1)

            @blk.vector
            def _(v):
                v.wait_ge(sE, 2)
                v.tensor_copy(mrp[:], stats[:]).then_inc(sE, 1)

            @blk.tensor
            def _(te):
                te.wait_ge(sE, 3)
                te.transpose(trp[:], mrp[:], identb[:]).then_inc(sE, 1)

            @blk.vector
            def _(v):
                v.wait_ge(sE, 4)
                v.tensor_copy(trs[:], trp[:]).then_inc(sE, 1)

            @blk.sync
            def _(sy):
                sy.wait_ge(sE, 5)
                sy.dma_start(out=rowdr, in_=trs[:]).then_inc(sXR, 16)
                sy.wait_ge(sXR, 16)
                sy.dma_start(out=mrow[:],
                             in_=rowdr.rearrange("a b -> (a b)").rearrange("(o x) -> o x", o=1)
                             ).then_inc(sXR, 16)

            @blk.tensor
            def _(te):
                te.wait_ge(sXR, 32)
                for k in range(8):
                    te.wait_ge(sB, k)
                    last = None
                    for half in range(2):
                        last = te.matmul(ps_bc[:, half * 512:(half + 1) * 512], onerow[0:1, :],
                                         mrow[:, k * 1024 + half * 512: k * 1024 + (half + 1) * 512],
                                         start=True, stop=True)
                    last.then_inc(sE, 1)

            @blk.scalar
            def _(sc):
                for k in range(8):
                    sc.wait_ge(sE, 5 + (k + 1))
                    sc.activation(mbc[:, k * 1024:(k + 1) * 1024], ps_bc[:],
                                  AF.Copy).then_inc(sB, 1)

        with p.fori(0, FT) as f:
            with nc.Block() as blk:
                @blk.vector
                def _(v):
                    v.wait_ge(sB, 8)
                    v.tensor_tensor(xbf[:, ts(f, T)], xbf[:, ts(f, T)], mbc[:, 0:T], ALU.subtract)
                    v.drain()
                    v.tensor_tensor(xbf[:, ts(f, T)], xbf[:, ts(f, T)], mbc[:, T:2 * T],
                                    ALU.mult).then_inc(sN, 1)
    return sN



def emit_mm_pass(p, tag, strips_dram, n_out, n_in, rhs_ap, psum, evac_cb, gate=None,
                 entry_gate=None):
    nc = p.nc
    sW = p.sem(tag + "w")
    sMM = p.sem(tag + "mm")
    sEV = p.sem(tag + "ev")
    with contextlib.ExitStack() as es:
        E = es.enter_context
        wbuf = E(nc.sbuf_tensor(tag + "_sb_w", [128, n_in * 128], BF16))

        with p.fori(0, n_out) as o:
            with nc.Block() as blk:
                @blk.gpsimd
                def _(g):
                    if entry_gate is not None:
                        g.wait_ge(entry_gate[0], entry_gate[1])
                    g.wait_ge(sMM, o)
                    g.dma_start(out=wbuf[:].rearrange("p (i m) -> p i m", i=n_in),
                                in_=strips_dram[o].rearrange("i p m -> p i m")
                                ).then_inc(sW, 16)

                @blk.tensor
                def _(te):
                    te.wait_ge(sW, (o + 1) * 16)
                    if gate is not None:
                        te.wait_ge(gate[0], gate[1])
                    te.wait_ge(sEV, o)
                    last = None
                    for i in range(n_in):
                        for q in range(QB):
                            last = te.matmul(psum[:, q * 512:(q + 1) * 512],
                                             wbuf[:, i * 128:(i + 1) * 128],
                                             rhs_ap(i, q),
                                             start=(i == 0), stop=(i == n_in - 1))
                    last.then_inc(sMM, 1)

                evac_cb(blk, o)
    return sMM, sEV



def emit_block_a(p, tag, xbf, xn_sem, x_in_dram, x_out_dram, w, cst, onecol, onerow):
    nc = p.nc
    NH = 16
    sW = p.sem(tag + "W")
    sQ = p.sem(tag + "Qp")
    sQE = p.sem(tag + "QE")
    sSC = p.sem(tag + "SC")
    sEX = p.sem(tag + "EX")
    sAV = p.sem(tag + "AV")
    sNO = p.sem(tag + "NO")
    sAO = p.sem(tag + "AO")

    ao_dram = nc.dram_tensor(tag + "_ao", [D, T], BF16, kind="Internal").ap()

    with contextlib.ExitStack() as es:
        E = es.enter_context
        wq = E(nc.sbuf_tensor(tag + "_sb_wq", [128, FT * 128], BF16))
        kb = E(nc.sbuf_tensor(tag + "_sb_kb", [128, T], BF16))
        vb = E(nc.sbuf_tensor(tag + "_sb_vb", [128, T], BF16))
        qsb = E(nc.sbuf_tensor(tag + "_sb_qsb", [128, T], BF16))
        pt = E(nc.sbuf_tensor(tag + "_sb_pt", [128, 32 * 512], BF16))
        rrow = E(nc.sbuf_tensor(tag + "_sb_rrow", [1, 512], F32))
        rbc = E(nc.sbuf_tensor(tag + "_sb_rbc", [128, 512], F32))
        aoc = E(nc.sbuf_tensor(tag + "_sb_aoc", [128, 512], BF16))
        psA = E(nc.psum_tensor(tag + "_psA", [128, 8 * 512], F32))

        with nc.Block() as b0:
            @b0.tensor
            def _(te):
                te.nop().then_inc(sAV, 1)
                te.nop().then_inc(sEX, 1)

        with p.fori(0, NH) as h:
            with nc.Block() as blk:
                @blk.gpsimd
                def _(g):
                    g.wait_ge(xn_sem, FT)
                    g.wait_ge(sAV, h * QB + 1)
                    g.wait_ge(sNO, h * QB * 3)
                    g.dma_start(out=wq[:].rearrange("p (i m) -> p i m", i=FT),
                                in_=w["wq_strips"][h].rearrange("i p m -> p i m")
                                ).then_inc(sW, 16)
                    g.dma_start(out=kb[:], in_=w["k_fm"][h]).then_inc(sW, 16)
                    g.dma_start(out=vb[:].rearrange("p (j m) -> p j m", j=32),
                                in_=w["v_tiles"][h].rearrange("j p m -> p j m")
                                ).then_inc(sW, 16)

                @blk.tensor
                def _(te):
                    te.wait_ge(sW, (h + 1) * 48)
                    te.wait_ge(xn_sem, FT)
                    te.wait_ge(sAV, h * QB + 1)
                    te.wait_ge(sNO, h * QB * 3)
                    last = None
                    for i in range(FT):
                        for q in range(QB):
                            last = te.matmul(psA[:, q * 512:(q + 1) * 512],
                                             wq[:, i * 128:(i + 1) * 128],
                                             xbf[:, ts(i, T)][:, q * 512:(q + 1) * 512],
                                             start=(i == 0), stop=(i == FT - 1))
                    last.then_inc(sQ, 1)

                @blk.scalar
                def _(sc):
                    sc.wait_ge(sQ, h + 1)
                    last = None
                    for k in range(4):
                        last = sc.activation(qsb[:, k * 1024:(k + 1) * 1024],
                                             psA[:, k * 1024:(k + 1) * 1024], AF.Copy,
                                             scale=cst[:, C_QSCALE:C_QSCALE + 1])
                    last.then_inc(sQE, 1)

            with p.fori(0, QB) as q:
                qq = h * QB + q
                with nc.Block() as blk:
                    @blk.tensor
                    def _(te):
                        te.wait_ge(sQE, h + 1)
                        te.wait_ge(sNO, qq * 3)
                        for j2 in range(16):
                            grp = j2 % 2
                            te.wait_ge(sEX, qq * 16 + j2)
                            last = None
                            for jj in range(2):
                                j = j2 * 2 + jj
                                last = te.matmul(
                                    psA[:, (grp * 2 + jj) * 512:(grp * 2 + jj + 1) * 512],
                                    kb[:, j * 128:(j + 1) * 128],
                                    qsb[:, ts(q, 512)],
                                    start=True, stop=True)
                            last.then_inc(sSC, 1)
                        last = None
                        for j in range(32):
                            if j % 2 == 0:
                                te.wait_ge(sEX, qq * 16 + j // 2 + 2)
                            te.matmul(psA[:, 4 * 512:5 * 512],
                                      vb[:, j * 128:(j + 1) * 128],
                                      pt[:, j * 512:(j + 1) * 512],
                                      start=(j == 0), stop=(j == 31))
                            last = te.matmul(psA[0:1, 5 * 512:6 * 512],
                                             onecol[:],
                                             pt[:, j * 512:(j + 1) * 512],
                                             start=(j == 0), stop=(j == 31))
                        last.then_inc(sAV, 1)

                    @blk.scalar
                    def _(sc):
                        for j2 in range(16):
                            sc.wait_ge(sSC, qq * 16 + j2 + 1)
                            grp = j2 % 2
                            sc.activation(pt[:, j2 * 1024:(j2 + 1) * 1024],
                                          psA[:, grp * 1024:(grp + 1) * 1024],
                                          AF.Exp).then_inc(sEX, 1)

                    @blk.vector
                    def _(v):
                        v.wait_ge(sAV, qq + 2)
                        v.reciprocal(rrow[:], psA[0:1, 5 * 512:6 * 512]).then_inc(sNO, 1)

                    @blk.tensor
                    def _(te):
                        te.wait_ge(sNO, qq * 3 + 1)
                        te.matmul(psA[:, 6 * 512:7 * 512], p.onerow_f[0:1, :], rrow[:],
                                  start=True, stop=True).then_inc(sNO, 1)

                    @blk.vector
                    def _(v):
                        v.wait_ge(sNO, qq * 3 + 2)
                        v.wait_ge(sAO, qq * 16)
                        v.tensor_copy(rbc[:], psA[:, 6 * 512:7 * 512])
                        v.drain()
                        v.tensor_tensor(aoc[:], psA[:, 4 * 512:5 * 512], rbc[:],
                                        ALU.mult).then_inc(sNO, 1)

                    @blk.sync
                    def _(sy):
                        sy.wait_ge(sNO, qq * 3 + 3)
                        sy.dma_start(out=ao_dram[ts(h, 128), ts(q, 512)],
                                     in_=aoc[:]).then_inc(sAO, 16)

    with contextlib.ExitStack() as es:
        E = es.enter_context
        aot = xbf
        x0t = E(nc.sbuf_tensor(tag + "_sb_x0t", [128, T], F32))
        x1t = E(nc.sbuf_tensor(tag + "_sb_x1t", [128, T], F32))
        psO = E(nc.psum_tensor(tag + "_psO", [128, 8 * 512], F32))
        sL = p.sem(tag + "L")
        sR = p.sem(tag + "R")
        sS = p.sem(tag + "S")
        sEVo = p.sem(tag + "WOev")
        sMMo = p.sem(tag + "WOmm")

        with nc.Block() as blk:
            @blk.sync
            def _(sy):
                sy.wait_ge(sAO, 16 * NH * QB)
                for i in range(FT):
                    sy.dma_start(out=aot[:, i * T:(i + 1) * T],
                                 in_=ao_dram[i * 128:(i + 1) * 128, :]).then_inc(sL, 16)

        def evac(blk, o):
            @blk.sync
            def _(sy):
                sy.dma_start(out=x0t[:], in_=x_in_dram[ts(o, 128), :]).then_inc(sR, 16)
                sy.wait_ge(sEVo, o + 1)
                sy.dma_start(out=x_out_dram[ts(o, 128), :], in_=x1t[:]).then_inc(sS, 16)

            @blk.vector
            def _(v):
                v.wait_ge(sMMo, o + 1)
                v.wait_ge(sR, (o + 1) * 16)
                v.wait_ge(sS, o * 16)
                last = None
                for k in range(8):
                    last = v.tensor_tensor(x1t[:, k * 512:(k + 1) * 512],
                                           psO[:, k * 512:(k + 1) * 512],
                                           x0t[:, k * 512:(k + 1) * 512], ALU.add)
                last.then_inc(sEVo, 1)

        emit_mm_pass(p, tag + "WO", w["wo_strips"], FT, FT,
                     lambda i, q: aot[:, i * T + q * 512: i * T + (q + 1) * 512],
                     psO, evac, gate=(sL, 16 * FT),
                     entry_gate=(sAO, 16 * 16 * QB))
    return sS



def emit_gu_loop(p, tag, n_out, base, gate_strips, up_strips, xbf, psum, gsb,
                 out_cb, nq=QB, toff=0, gate=None, n_in=FT, entry_gate=None,
                 sem_base=None):
    nc = p.nc
    sW = p.sem(tag + "W")
    sG = p.sem(tag + "G")
    sSi = p.sem(tag + "Si")
    sU = p.sem(tag + "U")
    sEV = p.sem(tag + "EV")
    sb = base if sem_base is None else sem_base
    p.uid = getattr(p, "uid", 0) + 1
    with contextlib.ExitStack() as es:
        E = es.enter_context
        wg = E(nc.sbuf_tensor(f"{tag}_sb_wg{p.uid}", [128, n_in * 128], BF16))
        wu = E(nc.sbuf_tensor(f"{tag}_sb_wu{p.uid}", [128, n_in * 128], BF16))

        with p.fori(0, n_out) as o_:
            u = base + o_
            us = sb + o_
            with nc.Block() as blk:
                @blk.gpsimd
                def _(g):
                    if entry_gate is not None:
                        g.wait_ge(entry_gate[0], entry_gate[1])
                    g.wait_ge(sU, us)
                    g.dma_start(out=wg[:].rearrange("p (i m) -> p i m", i=n_in),
                                in_=gate_strips[u].rearrange("i p m -> p i m")
                                ).then_inc(sW, 16)
                    g.dma_start(out=wu[:].rearrange("p (i m) -> p i m", i=n_in),
                                in_=up_strips[u].rearrange("i p m -> p i m")
                                ).then_inc(sW, 16)

                @blk.tensor
                def _(te):
                    te.wait_ge(sW, (us + 1) * 32)
                    if gate is not None:
                        te.wait_ge(gate[0], gate[1])
                    if entry_gate is not None:
                        te.wait_ge(entry_gate[0], entry_gate[1])
                    te.wait_ge(sEV, us)
                    last = None
                    for i in range(n_in):
                        for q in range(nq):
                            last = te.matmul(psum[:, q * 512:(q + 1) * 512],
                                             wg[:, i * 128:(i + 1) * 128],
                                             xbf[:, ts(i, T)][:, toff + q * 512: toff + (q + 1) * 512],
                                             start=(i == 0), stop=(i == n_in - 1))
                    last.then_inc(sG, 1)
                    te.wait_ge(sSi, us + 1)
                    last = None
                    for i in range(n_in):
                        for q in range(nq):
                            last = te.matmul(psum[:, q * 512:(q + 1) * 512],
                                             wu[:, i * 128:(i + 1) * 128],
                                             xbf[:, ts(i, T)][:, toff + q * 512: toff + (q + 1) * 512],
                                             start=(i == 0), stop=(i == n_in - 1))
                    last.then_inc(sU, 1)

                @blk.scalar
                def _(sc):
                    sc.wait_ge(sG, us + 1)
                    sc.wait_ge(sEV, us)
                    last = None
                    for k in range((nq + 1) // 2):
                        w0 = min(1024, nq * 512 - k * 1024)
                        last = sc.activation(gsb[:, k * 1024: k * 1024 + w0],
                                             psum[:, k * 1024: k * 1024 + w0], AF.Silu)
                    last.then_inc(sSi, 1)

                out_cb(blk, us, o_)
    return sEV



def emit_down_accum(p, tag, n_out, n_in, strips, rhs_tile, x_out_dram,
                    nq=QB, toff=0, gate=None, accum_gate=None, entry_gate=None,
                    sem_base=0):
    nc = p.nc
    sW = p.sem(tag + "W")
    sMM = p.sem(tag + "MM")
    sEV = p.sem(tag + "EV")
    sAC = p.sem(tag + "AC")
    p.uid = getattr(p, "uid", 0) + 1
    with contextlib.ExitStack() as es:
        E = es.enter_context
        wd = E(nc.sbuf_tensor(f"{tag}_sb_wd{p.uid}", [128, n_in * 128], BF16))
        yb = E(nc.sbuf_tensor(f"{tag}_sb_yb{p.uid}", [128, nq * 512], F32))

        with p.fori(0, n_out) as o:
            os_ = sem_base + o
            with nc.Block() as blk:
                @blk.gpsimd
                def _(g):
                    if entry_gate is not None:
                        g.wait_ge(entry_gate[0], entry_gate[1])
                    g.wait_ge(sMM, os_)
                    g.dma_start(out=wd[:].rearrange("p (i m) -> p i m", i=n_in),
                                in_=strips[o].rearrange("i p m -> p i m")
                                ).then_inc(sW, 16)

                @blk.tensor
                def _(te):
                    te.wait_ge(sW, (os_ + 1) * 16)
                    if gate is not None:
                        te.wait_ge(gate[0], gate[1])
                    te.wait_ge(sEV, os_)
                    last = None
                    for i in range(n_in):
                        for q in range(nq):
                            last = te.matmul(p.psDown[:, q * 512:(q + 1) * 512],
                                             wd[:, i * 128:(i + 1) * 128],
                                             rhs_tile(i)[:, q * 512:(q + 1) * 512],
                                             start=(i == 0), stop=(i == n_in - 1))
                    last.then_inc(sMM, 1)

                @blk.vector
                def _(v):
                    v.wait_ge(sMM, os_ + 1)
                    v.wait_ge(sAC, os_ * 16)
                    last = None
                    for k in range((nq + 1) // 2):
                        w0 = min(1024, nq * 512 - k * 1024)
                        last = v.tensor_copy(yb[:, k * 1024:k * 1024 + w0],
                                             p.psDown[:, k * 1024:k * 1024 + w0])
                    last.then_inc(sEV, 1)

                @blk.gpsimd
                def _(g):
                    g.wait_ge(sEV, os_ + 1)
                    if accum_gate is not None:
                        g.wait_ge(accum_gate[0], accum_gate[1])
                    g.dma_start(out=x_out_dram[ts(o, 128), toff:toff + nq * 512],
                                in_=yb[:], accum_op=ALU.add).then_inc(sAC, 16)
    return sAC



def emit_block_b(p, tag, xbf, x_in_dram, x_out_dram, w, cst, onecol, onerow,
                 gate=None):
    nc = p.nc
    sCP = p.sem(tag + "CP")
    with nc.Block() as blk:
        @blk.sync
        def _(sy):
            if gate is not None:
                sy.wait_ge(gate[0], gate[1])
            for f in range(FT):
                sy.dma_start(out=x_out_dram[ts(f, 128), :],
                             in_=x_in_dram[ts(f, 128), :]).then_inc(sCP, 16)

    sN = emit_stage_norm(p, "ST" + tag, x_in_dram, xbf, cst, p.identb, onecol,
                         onerow, gate=gate)

    ccin = nc.dram_tensor(tag + "_ccin", [16, 128], F32, kind="Internal")
    ccout = nc.dram_tensor(tag + "_ccout", [128, 128], F32, kind="Internal")

    sRTD = p.sem(tag + "RTD")
    sRT = p.sem(tag + "RT")
    sEP = p.sem(tag + "EP")
    sSC = p.sem(tag + "SC")
    sCCD = p.sem(tag + "CCD")
    sCCE = p.sem(tag + "CCE")
    sOF = p.sem(tag + "OF")
    sCB = p.sem(tag + "CB")
    NSTB = 7
    SEED = NSTB

    with contextlib.ExitStack() as e0:
        E0 = e0.enter_context
        embf = E0(nc.sbuf_tensor(tag + "_sb_embf", [16, T], BF16))

        with contextlib.ExitStack() as esR:
            ER = esR.enter_context
            rtw8 = ER(nc.sbuf_tensor(tag + "_sb_rtw8", [128, FT * 8], BF16))
            identf = ER(nc.sbuf_tensor(tag + "_sb_identf", [128, 128], F32))
            selb = ER(nc.sbuf_tensor(tag + "_sb_selb", [128, 16], F32))
            probs = ER(nc.sbuf_tensor(tag + "_sb_probs", [128, 8], F32))
            m8 = ER(nc.sbuf_tensor(tag + "_sb_m8", [128, 8], F32))
            sm1 = ER(nc.sbuf_tensor(tag + "_sb_sm1", [128, 8], F32))
            scr8 = ER(nc.sbuf_tensor(tag + "_sb_scr8", [128, 8], F32))
            tinI = ER(nc.sbuf_tensor(tag + "_sb_tinI", [128, 16], F32))
            tinW = ER(nc.sbuf_tensor(tag + "_sb_tinW", [128, 16], F32))
            emI = ER(nc.sbuf_tensor(tag + "_sb_emI", [16, T], F32))
            emW = ER(nc.sbuf_tensor(tag + "_sb_emW", [16, T], F32))
            cumI = ER(nc.sbuf_tensor(tag + "_sb_cumI", [16, T], F32))
            snd = ER(nc.sbuf_tensor(tag + "_sb_snd", [16, 128], F32))
            recvb = ER(nc.sbuf_tensor(tag + "_sb_recv", [128, 128], F32))
            offs = ER(nc.sbuf_tensor(tag + "_sb_offs", [16, 1], F32))
            psR = ER(nc.psum_tensor(tag + "_psR", [128, 8], F32))
            psTI = ER(nc.psum_tensor(tag + "_psTI", [16, 512], F32))
            psTW = ER(nc.psum_tensor(tag + "_psTW", [16, 512], F32))
            psOff = ER(nc.psum_tensor(tag + "_psOff", [16, 512], F32))

            with nc.Block() as blk:
                @blk.gpsimd
                def _(g):
                    g.wait_ge(sN, FT)
                    for f in range(FT):
                        g.dma_start(out=rtw8[:, f * 8:(f + 1) * 8],
                                    in_=w["rt8"][f]).then_inc(sRTD, 16)
                    g.dma_start(out=identf[:], in_=w["identf"]).then_inc(sRTD, 16)
                    g.dma_start(out=selb[:], in_=w["sel"]).then_inc(sRTD, 16)

                @blk.tensor
                def _(te):
                    te.nop().then_inc(sEP, SEED)

            for tt in range(TT):
                with nc.Block() as blk:
                    @blk.tensor
                    def _(te, tt=tt):
                        te.wait_ge(sRTD, 256)
                        te.wait_ge(sN, FT)
                        te.wait_ge(sEP, tt * NSTB + 2)
                        last = None
                        for f in range(FT):
                            last = te.matmul(psR[:], xbf[:, ts(f, T)][:, ts(tt, 128)],
                                             rtw8[:, f * 8:(f + 1) * 8],
                                             start=(f == 0), stop=(f == FT - 1))
                        last.then_inc(sRT, 1)

                    @blk.vector
                    def _(v, tt=tt):
                        b = SEED + tt * NSTB
                        v.wait_ge(sRT, tt + 1)
                        v.wait_ge(sEP, tt * NSTB + SEED)
                        v.reduce_max(sm1[:, 0:1], psR[:], axis=AX.X)
                        v.drain()
                        v.tensor_single_scalar(sm1[:, 1:2], sm1[:, 0:1], -1.0,
                                               ALU.mult).then_inc(sEP, 1)

                    @blk.scalar
                    def _(sc, tt=tt):
                        b = SEED + tt * NSTB
                        sc.wait_ge(sEP, b + 1)
                        sc.activation(probs[:], psR[:], AF.Exp, bias=sm1[:, 1:2],
                                      accum_out=sm1[:, 2:3]).then_inc(sEP, 1)

                    @blk.vector
                    def _(v, tt=tt):
                        b = SEED + tt * NSTB
                        v.wait_ge(sEP, b + 2)
                        v.reciprocal(sm1[:, 3:4], sm1[:, 2:3])
                        v.drain()
                        v.tensor_scalar(probs[:], probs[:], sm1[:, 3:4], None,
                                        ALU.mult).then_inc(sEP, 1)
                        v.wait_ge(sEP, b + 3)
                        v.max(m8[:], probs[:]).then_inc(sEP, 1)
                        v.wait_ge(sEP, b + 4)
                        v.tensor_scalar(tinI[:, 0:8], probs[:], m8[:, 0:1], None,
                                        ALU.is_equal)
                        v.drain()
                        v.tensor_scalar(tinI[:, 8:16], probs[:], m8[:, 1:2], None,
                                        ALU.is_equal)
                        v.drain()
                        v.tensor_scalar(scr8[:], tinI[:, 0:8], -1.0, 1.0,
                                        ALU.mult, ALU.add)
                        v.drain()
                        v.tensor_tensor(tinI[:, 8:16], tinI[:, 8:16], scr8[:],
                                        ALU.mult)
                        v.drain()
                        v.tensor_tensor(tinW[:, 0:8], probs[:], tinI[:, 0:8],
                                        ALU.mult)
                        v.drain()
                        v.tensor_tensor(tinW[:, 8:16], probs[:], tinI[:, 8:16],
                                        ALU.mult).then_inc(sEP, 1)

                    @blk.tensor
                    def _(te, tt=tt):
                        b = SEED + tt * NSTB
                        te.wait_ge(sEP, b + 5)
                        te.wait_ge(sRTD, 272)
                        te.transpose(psTI[:, 0:128], tinI[:], identf[:])
                        te.transpose(psTW[:, 0:128], tinW[:],
                                     identf[:]).then_inc(sEP, 1)

                    @blk.vector
                    def _(v, tt=tt):
                        b = SEED + tt * NSTB
                        v.wait_ge(sEP, b + 6)
                        v.tensor_copy(emI[:, ts(tt, 128)], psTI[:, 0:128])
                        v.tensor_copy(emW[:, ts(tt, 128)],
                                      psTW[:, 0:128]).then_inc(sEP, 1)

            with nc.Block() as blk:
                @blk.vector
                def _(v):
                    v.wait_ge(sEP, SEED + TT * NSTB)
                    v.tensor_tensor_scan(cumI[:], emI[:], emI[:], 0.0,
                                         ALU.add, ALU.bypass)
                    v.drain()
                    v.memset(snd[:], 0.0)
                    v.drain()
                    v.tensor_copy(snd[:, 0:1], cumI[:, T - 1:T]).then_inc(sSC, 1)

                @blk.gpsimd
                def _(g):
                    g.wait_ge(sSC, 1)
                    g.dma_start(out=ccin.ap(), in_=snd[:]).then_inc(sCCD, 16)
                    g.wait_ge(sCCD, 16)
                    g.collective_compute(
                        "AllGather", ALU.bypass,
                        replica_groups=[[0, 1, 2, 3, 4, 5, 6, 7]],
                        ins=[ccin.ap().opt()], outs=[ccout.ap().opt()],
                    ).then_inc(sCCE, 1)
                    g.wait_ge(sCCE, 1)
                    g.dma_start(out=recvb[:], in_=ccout.ap()).then_inc(sCCD, 16)

                @blk.tensor
                def _(te):
                    te.wait_ge(sCCD, 32)
                    te.wait_ge(sRTD, 288)
                    te.matmul(psOff[:, 0:1], selb[:], recvb[:, 0:1],
                              start=True, stop=True).then_inc(sOF, 1)

                @blk.vector
                def _(v):
                    v.wait_ge(sOF, 1)
                    v.tensor_copy(offs[:], psOff[:, 0:1])
                    v.drain()
                    v.tensor_scalar(cumI[:], cumI[:], offs[:, 0:1],
                                    cst[0:16, 7:8], ALU.add, ALU.is_le)
                    v.drain()
                    v.tensor_tensor(emW[:], emW[:], cumI[:], ALU.mult)
                    v.drain()
                    v.tensor_copy(embf[:], emW[:]).then_inc(sCB, 1)

            if hasattr(p, "dbg_bemI"):
                with nc.Block() as blk:
                    @blk.gpsimd
                    def _(g):
                        g.wait_ge(sCB, 1)
                        g.dma_start(out=p.dbg_bemI, in_=emI[:]).then_inc(sCCD, 16)
                        g.dma_start(out=p.dbg_bemW, in_=emW[:]).then_inc(sCCD, 16)

        sW5D = p.sem(tag + "W5D")
        sKB = p.sem(tag + "KB")
        HT = T // 2
        with contextlib.ExitStack() as e5:
            E5 = e5.enter_context
            mselb = E5(nc.sbuf_tensor(tag + "_sb_msel", [16, 1024], BF16))
            combb = E5(nc.sbuf_tensor(tag + "_sb_combb", [128, HT], BF16))
            gsbm = E5(nc.sbuf_tensor(tag + "_sb_gsbm", [128, HT], BF16))
            hbuf = E5(nc.sbuf_tensor(tag + "_sb_hbuf", [128, 11 * HT], BF16))
            psC = E5(nc.psum_tensor(tag + "_psC", [128, 512], F32))
            psGU = E5(nc.psum_tensor(tag + "_psGU", [128, 4 * 512], F32))
            p.psDown = E5(nc.psum_tensor(tag + "_psDn", [128, 2 * 512], F32))

            with nc.Block() as blk:
                @blk.gpsimd
                def _(g):
                    g.wait_ge(sCB, 1)
                    g.dma_start(out=mselb[:], in_=w["msel"]).then_inc(sW5D, 16)

            prev_gate = (sCP, 16 * FT)
            sGUEV = p.sem(tag + "GUEV")
            sDNMM = p.sem(tag + "DNMM")
            for half in range(2):
                for e in range(8):
                    idx = half * 8 + e
                    with nc.Block() as blk:
                        @blk.tensor
                        def _(te, idx=idx, e=e, half=half):
                            te.wait_ge(sCB, 1)
                            te.wait_ge(sW5D, 16)
                            te.wait_ge(sGUEV, idx * 11)
                            for q in range(4):
                                te.wait_ge(sKB, idx * 8 + q * 2)
                                te.matmul(psC[:], mselb[:, e * 128:(e + 1) * 128],
                                          embf[:, half * HT + q * 512:
                                               half * HT + (q + 1) * 512],
                                          start=True, stop=True).then_inc(sKB, 1)

                        @blk.vector
                        def _(v, idx=idx):
                            for q in range(4):
                                v.wait_ge(sKB, idx * 8 + q * 2 + 1)
                                v.tensor_copy(combb[:, ts(q, 512)],
                                              psC[:]).then_inc(sKB, 1)

                    def out_cb_b(blk, us, o_, idx=idx):
                        @blk.vector
                        def _(v):
                            v.wait_ge(p.sem(tag + "GUU"), us + 1)
                            v.wait_ge(sKB, (idx + 1) * 8)
                            v.wait_ge(sDNMM, idx * 32)
                            for k4 in range(4):
                                v.tensor_tensor(gsbm[:, ts(k4, 512)],
                                                psGU[:, ts(k4, 512)],
                                                gsbm[:, ts(k4, 512)], ALU.mult)
                            v.drain()
                            v.tensor_tensor(gsbm[:], gsbm[:], combb[:], ALU.mult)
                            v.drain()
                            v.tensor_copy(hbuf[:, ts(o_, HT)],
                                          gsbm[:]).then_inc(sGUEV, 1)

                    emit_gu_loop(p, tag + "GU", 11, e * 11, w["bg"], w["bu"],
                                 xbf, psGU, gsbm, out_cb_b, nq=4,
                                 toff=half * HT, gate=(sN, FT),
                                 entry_gate=(sCB, 1), sem_base=idx * 11)

                    for qh in range(2):
                        pno = idx * 2 + qh
                        ac = emit_down_accum(
                            p, tag + "DN", FT, 11, w["bd"][e],
                            lambda i, qh=qh: hbuf[:, ts(i, HT)][:, qh * 1024:(qh + 1) * 1024],
                            x_out_dram, nq=2, toff=half * HT + qh * 1024,
                            gate=(sGUEV, idx * 11 + 11),
                            accum_gate=prev_gate, entry_gate=(sCB, 1),
                            sem_base=pno * FT)
                        prev_gate = (ac, (pno + 1) * 16 * FT)
    return prev_gate



def emit_block_d(p, tag, xbf, x_in_dram, x_out_dram, w, cst, onecol, onerow, gate=None):
    nc = p.nc
    sCP = p.sem(tag + "CP")

    with nc.Block() as blk:
        @blk.sync
        def _(sy):
            if gate is not None:
                sy.wait_ge(gate[0], gate[1])
            for f in range(FT):
                sy.dma_start(out=x_out_dram[ts(f, 128), :],
                             in_=x_in_dram[ts(f, 128), :]).then_inc(sCP, 16)

    sN = emit_stage_norm(p, "ST" + tag, x_in_dram, xbf, cst, p.identb, onecol, onerow,
                         gate=gate)

    sMX = p.sem(tag + "MX")
    sMXD = p.sem(tag + "MXD")
    sRT = p.sem(tag + "RT")
    sRTD = p.sem(tag + "RTD")
    sEP = p.sem(tag + "EP")
    sWT = p.sem(tag + "WT")

    with contextlib.ExitStack() as es0:
        E0 = es0.enter_context
        mixes = E0(nc.sbuf_tensor(tag + "_sb_mixes", [128, 8], BF16))
        cem = E0(nc.sbuf_tensor(tag + "_sb_cem", [128, T], BF16))

        with contextlib.ExitStack() as esR:
            ER = esR.enter_context
            amp = ER(nc.sbuf_tensor(tag + "_sb_amp", [128, 16], F32))
            esr = ER(nc.sbuf_tensor(tag + "_sb_esr", [1, 128], F32))
            esbc_ps = ER(nc.psum_tensor(tag + "_esbcps", [128, 128], F32))
            esbc = ER(nc.sbuf_tensor(tag + "_sb_esbc", [128, 128], F32))
            mix1 = ER(nc.sbuf_tensor(tag + "_sb_mix1", [128, 9], F32))
            rtw = ER(nc.sbuf_tensor(tag + "_sb_rtw", [128, FT * 128], BF16))
            icolb = ER(nc.sbuf_tensor(tag + "_sb_icol", [128, 128], F32))
            psR = ER(nc.psum_tensor(tag + "_psR", [128, 128], F32))
            psTc = ER(nc.psum_tensor(tag + "_psTc", [128, 128], BF16))
            probs = ER(nc.sbuf_tensor(tag + "_sb_probs", [128, 128], F32))
            cbf = ER(nc.sbuf_tensor(tag + "_sb_cbf", [128, 128], BF16))
            sm1 = ER(nc.sbuf_tensor(tag + "_sb_sm1", [128, 8], F32))
            m8 = ER(nc.sbuf_tensor(tag + "_sb_m8", [128, 8], F32))
            idxu = ER(nc.sbuf_tensor(tag + "_sb_idxu", [128, 8], U16))
            idxf = ER(nc.sbuf_tensor(tag + "_sb_idxf", [128, 8], F32))
            msk = ER(nc.sbuf_tensor(tag + "_sb_msk", [128, 128], F32))
            msk2 = ER(nc.sbuf_tensor(tag + "_sb_msk2", [128, 128], F32))

            with nc.Block() as blk:
                @blk.gpsimd
                def _(g):
                    g.wait_ge(sN, FT)
                    g.dma_start(out=amp[:], in_=w["amp"]).then_inc(sMXD, 16)
                    g.dma_start(out=esr[:], in_=w["escal_row"]).then_inc(sMXD, 16)
                    g.dma_start(out=icolb[:], in_=w["icol"]).then_inc(sMXD, 16)
                    for f in range(FT):
                        g.dma_start(out=rtw[:, f * 128:(f + 1) * 128],
                                    in_=w["router"][f]).then_inc(sRTD, 16)

                @blk.scalar
                def _(sc):
                    sc.wait_ge(sMXD, 48)
                    sc.activation(amp[:], amp[:], AF.Square).then_inc(sMX, 1)

                @blk.vector
                def _(v):
                    v.wait_ge(sMX, 1)
                    v.tensor_tensor(mix1[:, 0:8], amp[:, 0:16:2], amp[:, 1:16:2], ALU.add)
                    v.drain()
                    v.reduce_sum(mix1[:, 8:9], mix1[:, 0:8], axis=AX.X)
                    v.drain()
                    v.tensor_single_scalar(mix1[:, 8:9], mix1[:, 8:9], 1e-8, ALU.add)
                    v.drain()
                    v.reciprocal(mix1[:, 8:9], mix1[:, 8:9])
                    v.drain()
                    v.tensor_scalar(mix1[:, 0:8], mix1[:, 0:8], mix1[:, 8:9], None, ALU.mult)
                    v.drain()
                    v.tensor_copy(mixes[:], mix1[:, 0:8]).then_inc(sMX, 1)

                @blk.tensor
                def _(te):
                    te.wait_ge(sMXD, 48)
                    te.wait_ge(p.sem("CST"), 1)
                    te.wait_ge(p.sem("CSTD"), 16)
                    te.matmul(esbc_ps[:], p.onerow_f[0:1, :], esr[:],
                              start=True, stop=True).then_inc(sMX, 1)

                @blk.vector
                def _(v):
                    v.wait_ge(sMX, 3)
                    v.tensor_copy(esbc[:], esbc_ps[:]).then_inc(sMX, 1)

                @blk.tensor
                def _(te):
                    te.nop().then_inc(sEP, 9)

            NST = 9
            SEED = 9

            for tt in range(TT):
                with nc.Block() as blk:
                    @blk.tensor
                    def _(te):
                        te.wait_ge(sRTD, 16 * FT)
                        te.wait_ge(sN, FT)
                        te.wait_ge(sEP, tt * NST + 2)
                        last = None
                        for f in range(FT):
                            last = te.matmul(psR[:], xbf[:, ts(f, T)][:, ts(tt, 128)],
                                             rtw[:, f * 128:(f + 1) * 128],
                                             start=(f == 0), stop=(f == FT - 1))
                        last.then_inc(sRT, 1)

                    @blk.vector
                    def _(v):
                        v.wait_ge(sRT, tt + 1)
                        v.wait_ge(sEP, tt * NST + SEED)
                        v.reduce_max(sm1[:, 0:1], psR[:], axis=AX.X)
                        v.tensor_single_scalar(sm1[:, 1:2], sm1[:, 0:1], -1.0,
                                               ALU.mult).then_inc(sEP, 1)

                    @blk.scalar
                    def _(sc):
                        sc.wait_ge(sEP, tt * NST + SEED + 1)
                        sc.activation(probs[:], psR[:], AF.Exp, bias=sm1[:, 1:2],
                                      accum_out=sm1[:, 2:3]).then_inc(sEP, 1)

                    @blk.vector
                    def _(v):
                        v.wait_ge(sEP, tt * NST + SEED + 2)
                        v.reciprocal(sm1[:, 3:4], sm1[:, 2:3])
                        v.drain()
                        v.tensor_scalar(probs[:], probs[:], sm1[:, 3:4], None,
                                        ALU.mult).then_inc(sEP, 1)
                        v.wait_ge(sEP, tt * NST + SEED + 3)
                        v.max(m8[:], probs[:]).then_inc(sEP, 1)
                        v.wait_ge(sEP, tt * NST + SEED + 4)
                        v.max_index(idxu[:], m8[:], probs[:]).then_inc(sEP, 1)

                    @blk.scalar
                    def _(sc):
                        sc.wait_ge(sEP, tt * NST + SEED + 5)
                        sc.copy(idxf[:], idxu[:]).then_inc(sEP, 1)

                    @blk.vector
                    def _(v):
                        v.wait_ge(sEP, tt * NST + SEED + 6)
                        v.tensor_scalar(msk[:], icolb[:], idxf[:, 0:1], None, ALU.is_equal)
                        for j in range(1, 4):
                            v.tensor_scalar(msk2[:], icolb[:], idxf[:, j:j + 1], None,
                                            ALU.is_equal)
                            v.drain()
                            v.tensor_tensor(msk[:], msk[:], msk2[:], ALU.add)
                            v.drain()
                        v.tensor_tensor(msk[:], msk[:], probs[:], ALU.mult)
                        v.reduce_sum(sm1[:, 4:5], m8[:, 0:4], axis=AX.X)
                        v.drain()
                        v.tensor_single_scalar(sm1[:, 4:5], sm1[:, 4:5], 1e-8, ALU.add)
                        v.drain()
                        v.reciprocal(sm1[:, 5:6], sm1[:, 4:5])
                        v.drain()
                        v.tensor_scalar(msk[:], msk[:], sm1[:, 5:6], None, ALU.mult)
                        v.drain()
                        v.tensor_tensor(cbf[:], msk[:], esbc[:], ALU.mult).then_inc(sEP, 1)

                    @blk.tensor
                    def _(te):
                        te.wait_ge(sEP, tt * NST + SEED + 7)
                        te.transpose(psTc[:], cbf[:], p.identb[:]).then_inc(sEP, 1)

                    @blk.vector
                    def _(v):
                        v.wait_ge(sEP, tt * NST + SEED + 8)
                        v.tensor_copy(cem[:, ts(tt, 128)], psTc[:]).then_inc(sEP, 1)

        sMR = p.sem(tag + "MR")
        sMRD = p.sem(tag + "MRD")
        mixes_rep = E0(nc.sbuf_tensor(tag + "_sb_mxrep", [128, 1024], BF16))
        with contextlib.ExitStack() as esM:
            EM = esM.enter_context
            psT8 = EM(nc.psum_tensor(tag + "_psT8", [8, 128], BF16))
            mixT = EM(nc.sbuf_tensor(tag + "_sb_mixT", [8, 128], BF16))
            psRep = EM(nc.psum_tensor(tag + "_psRep", [128, 1024], F32))
            repb = EM(nc.sbuf_tensor(tag + "_sb_rep", [8, 1024], BF16))
            with nc.Block() as blk:
                @blk.gpsimd
                def _(g):
                    g.wait_ge(sEP, SEED + TT * 9)
                    g.dma_start(out=repb[:], in_=w["rep"]).then_inc(sMRD, 16)

                @blk.tensor
                def _(te):
                    te.wait_ge(sEP, SEED + TT * 9)
                    te.wait_ge(sMX, 2)
                    te.wait_ge(p.sem("CST"), 1)
                    te.wait_ge(p.sem("CSTD"), 16)
                    te.transpose(psT8[:], mixes[:], p.identb[:]).then_inc(sMR, 1)

                @blk.vector
                def _(v):
                    v.wait_ge(sMR, 1)
                    v.tensor_copy(mixT[:], psT8[:]).then_inc(sMR, 1)

                @blk.tensor
                def _(te):
                    te.wait_ge(sMR, 2)
                    te.wait_ge(sMRD, 16)
                    te.matmul(psRep[:, 0:512], mixT[:], repb[:, 0:512],
                              start=True, stop=True)
                    te.matmul(psRep[:, 512:1024], mixT[:], repb[:, 512:1024],
                              start=True, stop=True).then_inc(sMR, 1)

                @blk.vector
                def _(v):
                    v.wait_ge(sMR, 3)
                    v.tensor_copy(mixes_rep[:], psRep[:]).then_inc(sMR, 1)

                @blk.gpsimd
                def _(g):
                    if hasattr(p, "dbg_cem"):
                        g.wait_ge(sMR, 4)
                        g.dma_start(out=p.dbg_cem, in_=cem[:]).then_inc(sMRD, 16)
                        g.dma_start(out=p.dbg_mxr, in_=mixes_rep[:]).then_inc(sMRD, 16)

        D_PARTS = globals().get("DPARTS", "both")
        prev_gate = (sCP, 16 * FT)
        entry_gate = (sMR, 4)
        QT = T // 4
        for part in (range(4) if D_PARTS in ("both", "shared") else []):
            with contextlib.ExitStack() as es1:
                E1 = es1.enter_context
                sh = E1(nc.sbuf_tensor(f"{tag}_sb_sh{part}", [128, FT * QT], BF16))
                gsb = E1(nc.sbuf_tensor(f"{tag}_sb_gsbs{part}", [128, QT], BF16))
                psGU = E1(nc.psum_tensor(f"{tag}_psGU{part}", [128, 2 * 512], F32))
                tg = f"{tag}SH{part}"
                shv = sh.rearrange("p (i t) -> p i t", i=FT)

                def out_cb(blk, u, o_, sh=sh, gsb=gsb, psGU=psGU, tg=tg, QT=QT):
                    @blk.vector
                    def _(v):
                        v.wait_ge(p.sem(tg + "U"), u + 1)
                        last = None
                        for k in range(2):
                            last = v.tensor_tensor(sh[:, ts(o_ * 2 + k, 512)],
                                                   psGU[:, k * 512:(k + 1) * 512],
                                                   gsb[:, k * 512:(k + 1) * 512], ALU.mult)
                        last.then_inc(p.sem(tg + "EV"), 1)

                emit_gu_loop(p, tg, FT, 0, w["shg_strips"], w["shu_strips"],
                             xbf, psGU, gsb, out_cb, nq=2, toff=part * QT,
                             gate=(sN, FT), entry_gate=entry_gate)

                if part == 0 and hasattr(p, "dbg_sh"):
                    with nc.Block() as blkD:
                        @blkD.gpsimd
                        def _(g):
                            g.wait_ge(p.sem(tg + "EV"), FT)
                            g.dma_start(out=p.dbg_sh, in_=sh[:]).then_inc(p.sem(tg + "DBG"), 16)
                p.psDown = E1(nc.psum_tensor(f"{tag}_psDnS{part}", [128, 2 * 512], F32))
                ac = emit_down_accum(p, f"{tag}SD{part}", FT, FT, w["shd_strips"],
                                     lambda i, shv=shv: shv[:, i, :],
                                     x_out_dram, nq=2, toff=part * QT,
                                     gate=(p.sem(tg + "EV"), FT),
                                     accum_gate=prev_gate, entry_gate=entry_gate)
                prev_gate = (ac, 16 * FT)
                entry_gate = (ac, 16 * FT)

        HT = T // 2
        for half in (range(2) if D_PARTS in ("both", "moe") else []):
            with contextlib.ExitStack() as es2:
                E2 = es2.enter_context
                mixed = E2(nc.sbuf_tensor(f"{tag}_sb_mixed{half}", [128, 8 * HT], BF16))
                wtbc = E2(nc.sbuf_tensor(f"{tag}_sb_wtbc{half}", [128, HT], BF16))
                gsbm = E2(nc.sbuf_tensor(f"{tag}_sb_gsbm{half}", [128, HT], BF16))
                psM = E2(nc.psum_tensor(f"{tag}_psM{half}", [128, 4 * 512], F32))
                psB = E2(nc.psum_tensor(f"{tag}_psB{half}", [128, 512], F32))
                sKB = p.sem(f"{tag}KB{half}")
                sMS = p.sem(f"{tag}MS{half}")
                tgm = f"{tag}MO{half}"

                with nc.Block() as blk:
                    @blk.vector
                    def _(v):
                        v.wait_ge(entry_gate[0], entry_gate[1])
                        v.memset(mixed[:], 0.0).then_inc(sMS, 1)

                for kk in range(8):
                    NB = 4 * 2
                    with nc.Block() as blk:
                        @blk.tensor
                        def _(te, kk=kk):
                            te.wait_ge(entry_gate[0], entry_gate[1])
                            te.wait_ge(sMR, 4)
                            te.wait_ge(p.sem(tgm + "EV"), kk * 8)
                            for q in range(4):
                                te.wait_ge(sKB, kk * NB + q * 2)
                                te.matmul(psB[:], mixes_rep[:, kk * 128:(kk + 1) * 128],
                                          cem[:, half * HT + q * 512: half * HT + (q + 1) * 512],
                                          start=True, stop=True).then_inc(sKB, 1)

                        @blk.vector
                        def _(v, kk=kk):
                            for q in range(4):
                                v.wait_ge(sKB, kk * NB + q * 2 + 1)
                                v.tensor_copy(wtbc[:, ts(q, 512)], psB[:]).then_inc(sKB, 1)

                    def out_cb_m(blk, u, o_, kk=kk, half=half, tgm=tgm, sKB=sKB,
                                 sMS=sMS, psM=psM, gsbm=gsbm, wtbc=wtbc, mixed=mixed,
                                 NB=NB):
                        @blk.vector
                        def _(v):
                            v.wait_ge(p.sem(tgm + "U"), u + 1)
                            v.wait_ge(sKB, (kk + 1) * NB)
                            v.wait_ge(sMS, 1)
                            for k4 in range(4):
                                v.tensor_tensor(gsbm[:, ts(k4, 512)], psM[:, ts(k4, 512)],
                                                gsbm[:, ts(k4, 512)], ALU.mult)
                            v.drain()
                            v.tensor_tensor(gsbm[:], gsbm[:], wtbc[:], ALU.mult)
                            v.drain()
                            last = None
                            for k4 in range(4):
                                last = v.tensor_tensor(mixed[:, ts(o_ * 4 + k4, 512)],
                                                       mixed[:, ts(o_ * 4 + k4, 512)],
                                                       gsbm[:, ts(k4, 512)], ALU.add)
                            last.then_inc(p.sem(tgm + "EV"), 1)

                    emit_gu_loop(p, tgm, 8, kk * 8, w["g_strips"], w["u_strips"],
                                 xbf, psM, gsbm, out_cb_m, nq=4, toff=half * HT,
                                 gate=(sN, FT), entry_gate=entry_gate)

                p.psDown = E2(nc.psum_tensor(f"{tag}_psDnM{half}", [128, 2 * 512], F32))
                mxv = mixed.rearrange("p (i t) -> p i t", i=8)
                for qh in range(2):
                    ac = emit_down_accum(p, f"{tag}MD{half}{qh}", FT, 8, w["d_strips"],
                                         lambda i, mxv=mxv, qh=qh: mxv[:, i, :][:, qh * 1024:(qh + 1) * 1024],
                                         x_out_dram, nq=2, toff=half * HT + qh * 1024,
                                         gate=(p.sem(tgm + "EV"), 64),
                                         accum_gate=prev_gate, entry_gate=entry_gate)
                    prev_gate = (ac, 16 * FT)
                entry_gate = (ac, 16 * FT)
    return prev_gate



def emit_block_c(p, tag, xbf, x_in_dram, x_out_dram, w, cst, onecol, onerow, gate=None):
    nc = p.nc
    NH, DK, DV, CH = 4, 256, 512, 64
    NCH = T // CH

    sN = emit_stage_norm(p, "ST" + tag, x_in_dram, xbf, cst, p.identb, onecol, onerow,
                         gate=gate)

    qd = nc.dram_tensor(tag + "_qd", [128, 8 * T], BF16, kind="Internal").ap()
    kd = nc.dram_tensor(tag + "_kd", [128, 8 * T], BF16, kind="Internal").ap()
    ktd = nc.dram_tensor(tag + "_ktd", [T, 1024], BF16, kind="Internal").ap()
    vd = nc.dram_tensor(tag + "_vd", [T, 2048], BF16, kind="Internal").ap()
    gd = nc.dram_tensor(tag + "_gd", [T, 2048], BF16, kind="Internal").ap()
    od = nc.dram_tensor(tag + "_od", [NH * T, DV], BF16, kind="Internal").ap()
    odv = od.rearrange("(hh t) d -> hh t d", hh=NH)
    ccin = nc.dram_tensor(tag + "_ccin", [128, 4096], F32, kind="Internal")
    ccout = nc.dram_tensor(tag + "_ccout", [256, 4096], F32, kind="Internal")

    CSTOP = globals().get("CSTOP", "all")
    sT = p.sem(tag + "T")
    sQK = p.sem(tag + "QK")
    sKT = p.sem(tag + "KT")
    sKC = p.sem(tag + "KC")
    sQS = p.sem(tag + "QS")

    with contextlib.ExitStack() as esc:
        Ec = esc.enter_context
        gkc = Ec(nc.sbuf_tensor(tag + "_sb_gk", [64, NH], F32))
        ginc = Ec(nc.sbuf_tensor(tag + "_sb_gin", [64, NH], F32))
        grev = Ec(nc.sbuf_tensor(tag + "_sb_grev", [64, NH * 64], F32))
        dmT = Ec(nc.sbuf_tensor(tag + "_sb_dmT", [64, NH * 64], BF16))
        gCc = Ec(nc.sbuf_tensor(tag + "_sb_gC", [128, NH], F32))
        par = Ec(nc.sbuf_tensor(tag + "_sb_par", [128, 1], F32))
        gnw = Ec(nc.sbuf_tensor(tag + "_sb_gnw", [1, 2048], F32))

        with nc.Block() as blk:
            @blk.gpsimd
            def _(g):
                g.wait_ge(sN, FT)
                g.dma_start(out=gkc[:], in_=w["gk"]).then_inc(sT, 16)
                g.dma_start(out=ginc[:], in_=w["gin"]).then_inc(sT, 16)
                g.dma_start(out=grev[:], in_=w["grev"]).then_inc(sT, 16)
                g.dma_start(out=dmT[:], in_=w["dmT"]).then_inc(sT, 16)
                g.dma_start(out=gCc[:], in_=w["gC"]).then_inc(sT, 16)
                g.dma_start(out=par[:], in_=w["parity"]).then_inc(sT, 16)
                g.dma_start(out=gnw[:], in_=w["gnw"]).then_inc(sT, 16)

        with contextlib.ExitStack() as e1:
            E1 = e1.enter_context
            rc = E1(nc.sbuf_tensor(tag + "_sb_rc", [128, T], F32))
            rs = E1(nc.sbuf_tensor(tag + "_sb_rs", [128, T], F32))
            wqs = E1(nc.sbuf_tensor(tag + "_sb_wqs", [128, 2 * FT * 128], BF16))
            ta = E1(nc.sbuf_tensor(tag + "_sb_ta", [128, 1024], F32))
            tb = E1(nc.sbuf_tensor(tag + "_sb_tb", [128, 1024], F32))
            qo = E1(nc.sbuf_tensor(tag + "_sb_qo", [128, 2 * 1024], BF16))
            ktb = E1(nc.sbuf_tensor(tag + "_sb_ktb", [64, 2 * 16 * 128], BF16))
            psq = E1(nc.psum_tensor(tag + "_psq", [128, 4 * 512], F32))
            psT = E1(nc.psum_tensor(tag + "_psT", [64, 2 * 1024], BF16))

            with nc.Block() as blk:
                @blk.gpsimd
                def _(g):
                    g.wait_ge(sN, FT)
                    g.dma_start(out=rc[:], in_=w["ropecos"]).then_inc(sT, 16)
                    g.dma_start(out=rs[:], in_=w["ropesin"]).then_inc(sT, 16)

            QT1 = 1024
            store_cnt = 0
            for u in range(32):
                qk, h, tq = u // 16, (u // 4) % 4, u % 4
                strips = w["wq_strips"] if qk == 0 else w["wk_strips"]
                dst = qd if qk == 0 else kd
                sc16 = (1.0 / 16.0) if qk == 0 else 1.0
                new_strip = (tq == 0)
                sbase = store_cnt
                nst = 2 if qk == 0 else 3
                store_cnt += nst * 16
                ku = u - 16
                kbT = ku * 16 if qk == 1 else 0
                kbC = ku * 32 if qk == 1 else 0
                with nc.Block() as blk:
                    if new_strip:
                        @blk.gpsimd
                        def _(g, h=h, u=u, strips=strips):
                            g.wait_ge(sQK, max(0, 2 * u - 1))
                            g.dma_start(
                                out=wqs[:].rearrange("p (o i m) -> p o i m", o=2, i=FT),
                                in_=strips[2 * h:2 * h + 2].rearrange("o i p m -> p o i m")
                                ).then_inc(sT, 16)

                    @blk.tensor
                    def _(te, u=u, tq=tq, qk=qk, kbT=kbT, kbC=kbC):
                        te.wait_ge(sT, 16 * (9 + (u // 4) + 1))
                        te.wait_ge(sQK, 2 * u)
                        last = None
                        for half in range(2):
                            for i in range(FT):
                                for q2 in range(2):
                                    last = te.matmul(
                                        psq[:, (half * 2 + q2) * 512:(half * 2 + q2 + 1) * 512],
                                        wqs[:, (half * FT + i) * 128:(half * FT + i + 1) * 128],
                                        xbf[:, ts(i, T)][:, tq * QT1 + q2 * 512: tq * QT1 + (q2 + 1) * 512],
                                        start=(i == 0), stop=(i == FT - 1))
                        last.then_inc(sQK, 1)
                        if qk == 1:
                            te.wait_ge(sQK, 2 * u + 2)
                            for c in range(16):
                                pp = c % 2
                                te.wait_ge(sKC, kbC + max(0, 2 * (c - 1)))
                                te.transpose(psT[:, pp * 1024:pp * 1024 + 128],
                                             qo[:, c * 64:(c + 1) * 64], p.identb[:])
                                te.transpose(psT[:, pp * 1024 + 128:pp * 1024 + 256],
                                             qo[:, 1024 + c * 64:1024 + (c + 1) * 64],
                                             p.identb[:]).then_inc(sKT, 1)

                    @blk.vector
                    def _(v, u=u, tq=tq, sc16=sc16, sbase=sbase, qk=qk, kbT=kbT, kbC=kbC):
                        v.wait_ge(sQK, 2 * u + 1)
                        v.wait_ge(sQS, sbase)
                        if u >= 1 and (u - 1) // 16 == 1:
                            v.wait_ge(sKT, (u - 1 - 16) * 16 + 16)
                        cs = rc[:, tq * QT1:(tq + 1) * QT1]
                        sn = rs[:, tq * QT1:(tq + 1) * QT1]
                        lo = psq[:, 0:1024]
                        hi = psq[:, 1024:2048]
                        v.scalar_tensor_tensor(ta[:], lo, sc16, cs, ALU.mult, ALU.mult)
                        v.scalar_tensor_tensor(tb[:], hi, sc16, sn, ALU.mult, ALU.mult)
                        v.drain()
                        v.tensor_tensor(qo[:, 0:1024], ta[:], tb[:], ALU.subtract)
                        v.drain()
                        v.scalar_tensor_tensor(ta[:], hi, sc16, cs, ALU.mult, ALU.mult)
                        v.scalar_tensor_tensor(tb[:], lo, sc16, sn, ALU.mult, ALU.mult)
                        v.drain()
                        v.tensor_tensor(qo[:, 1024:2048], ta[:], tb[:],
                                        ALU.add).then_inc(sQK, 1)
                        if qk == 1:
                            for c in range(16):
                                pp = c % 2
                                v.wait_ge(sKT, kbT + c + 1)
                                v.tensor_copy(ktb[:, (2 * c) * 128:(2 * c + 1) * 128],
                                              psT[:, pp * 1024:pp * 1024 + 128])
                                v.tensor_copy(ktb[:, (2 * c + 1) * 128:(2 * c + 2) * 128],
                                              psT[:, pp * 1024 + 128:pp * 1024 + 256]
                                              ).then_inc(sKC, 2)

                    @blk.sync
                    def _(sy, h=h, tq=tq, u=u, dst=dst, nst=nst, kbC=kbC, qk=qk):
                        sy.wait_ge(sQK, 2 * u + 2)
                        sy.dma_start(out=dst[:, (2 * h) * T + tq * QT1:(2 * h) * T + (tq + 1) * QT1],
                                     in_=qo[:, 0:1024]).then_inc(sQS, 16)
                        sy.dma_start(out=dst[:, (2 * h + 1) * T + tq * QT1:(2 * h + 1) * T + (tq + 1) * QT1],
                                     in_=qo[:, 1024:2048]).then_inc(sQS, 16)
                        if qk == 1:
                            sy.wait_ge(sKC, kbC + 32)
                            sy.dma_start(
                                out=ktd[tq * QT1:(tq + 1) * QT1, h * 256:(h + 1) * 256]
                                    .rearrange("(c j) d -> j c d", j=64),
                                in_=ktb[:].rearrange("j (c d) -> j c d", c=16)
                                ).then_inc(sQS, 16)

        if CSTOP == "c1a":
            return (sQS, store_cnt)
        prev_sVS = None
        for vg in range(2):
            sVW = p.sem(tag + f"VW{vg}")
            sVP = p.sem(tag + f"VP{vg}")
            sVE = p.sem(tag + f"VE{vg}")
            sVS = p.sem(tag + f"VS{vg}")
            wsrc = w["wvT"] if vg == 0 else w["wgT"]
            ddst = vd if vg == 0 else gd
            egate = (sQS, store_cnt) if vg == 0 else (prev_sVS, TT * 16)
            with contextlib.ExitStack() as e2:
                E2 = e2.enter_context
                wvs = E2(nc.sbuf_tensor(tag + f"_sb_wvs{vg}", [128, FT * 2048], BF16))
                vsb = E2(nc.sbuf_tensor(tag + f"_sb_vsb{vg}", [128, 2048], BF16))
                psv = E2(nc.psum_tensor(tag + f"_psv{vg}", [128, 4 * 512], F32))

                with nc.Block() as blk:
                    @blk.gpsimd
                    def _(g, wsrc=wsrc, egate=egate):
                        g.wait_ge(egate[0], egate[1])
                        for i in range(FT):
                            g.dma_start(out=wvs[:, i * 2048:(i + 1) * 2048],
                                        in_=wsrc[i]).then_inc(sVW, 16)

                for tt in range(TT):
                    with nc.Block() as blk:
                        @blk.tensor
                        def _(te, tt=tt):
                            te.wait_ge(sVW, 16 * FT)
                            te.wait_ge(sVE, tt)
                            last = None
                            for i in range(FT):
                                for dvb in range(4):
                                    last = te.matmul(psv[:, dvb * 512:(dvb + 1) * 512],
                                                     xbf[:, ts(i, T)][:, ts(tt, 128)],
                                                     wvs[:, i * 2048 + dvb * 512: i * 2048 + (dvb + 1) * 512],
                                                     start=(i == 0), stop=(i == FT - 1))
                            last.then_inc(sVP, 1)

                        if vg == 0:
                            @blk.vector
                            def _(v, tt=tt):
                                v.wait_ge(sVP, tt + 1)
                                v.wait_ge(sVS, tt * 16)
                                v.tensor_copy(vsb[:], psv[:]).then_inc(sVE, 1)
                        else:
                            @blk.scalar
                            def _(sc, tt=tt):
                                sc.wait_ge(sVP, tt + 1)
                                sc.wait_ge(sVS, tt * 16)
                                sc.activation(vsb[:, 0:1024], psv[:, 0:1024], AF.Silu)
                                sc.activation(vsb[:, 1024:2048], psv[:, 1024:2048],
                                              AF.Silu).then_inc(sVE, 1)

                        @blk.sync
                        def _(sy, tt=tt, ddst=ddst):
                            sy.wait_ge(sVE, tt + 1)
                            sy.dma_start(out=ddst[ts(tt, 128), :], in_=vsb[:]).then_inc(sVS, 16)
            prev_sVS = sVS

        if CSTOP == "c1b":
            return (prev_sVS, TT * 16)
        with contextlib.ExitStack() as eS:
            ES = eS.enter_context
            S_sb = ES(nc.sbuf_tensor(tag + "_sb_S", [128, NH * 1024], F32))
            S_bf = ES(nc.sbuf_tensor(tag + "_sb_Sbf", [128, NH * 1024], BF16))

            sH2 = p.sem(tag + "H2")
            sH3 = p.sem(tag + "H3")
            sHD = p.sem(tag + "HD")
            sCC = p.sem(tag + "CC")
            sCE = p.sem(tag + "CE")
            with contextlib.ExitStack() as e3:
                E3 = e3.enter_context
                ktcs = E3(nc.sbuf_tensor(tag + "_sb_ktcs", [64, 1024], BF16))
                vcs = E3(nc.sbuf_tensor(tag + "_sb_vcs", [64, 2048], BF16))
                ksc = E3(nc.sbuf_tensor(tag + "_sb_ksc", [64, 256], BF16))
                zlh = E3(nc.sbuf_tensor(tag + "_sb_zlh", [64, 128], BF16))
                zrh = E3(nc.sbuf_tensor(tag + "_sb_zrh", [64, 512], BF16))
                shs = E3(nc.sbuf_tensor(tag + "_sb_shs", [128, 4096], F32))
                psH = E3(nc.psum_tensor(tag + "_psH", [128, 8 * 512], F32))

                with nc.Block() as blk:
                    @blk.vector
                    def _(v):
                        v.wait_ge(prev_sVS, TT * 16)
                        v.memset(zlh[:], 0.0)
                        v.memset(zrh[:], 0.0).then_inc(sCE, 1)

                    @blk.tensor
                    def _(te):
                        te.wait_ge(sCE, 1)
                        last = None
                        for b in range(8):
                            last = te.matmul(psH[:, b * 512:(b + 1) * 512], zlh[:], zrh[:],
                                             start=True, stop=False)
                        last.then_inc(sCE, 1)

                with p.fori(0, NCH) as ch:
                    with nc.Block() as blk:
                        @blk.sync
                        def _(sy):
                            sy.wait_ge(sH3, ch * NH)
                            sy.dma_start(out=ktcs[:], in_=ktd[ts(ch, 64), :]).then_inc(sHD, 16)
                            sy.dma_start(out=vcs[:], in_=vd[ts(ch, 64), :]).then_inc(sHD, 16)

                        @blk.vector
                        def _(v):
                            for h in range(NH):
                                v.wait_ge(sHD, (ch + 1) * 32)
                                v.wait_ge(sH3, ch * NH + h)
                                v.tensor_scalar(ksc[:], ktcs[:, h * 256:(h + 1) * 256],
                                                grev[:, ts(h * 64 + ch, 1)], None,
                                                ALU.mult).then_inc(sH2, 1)

                        @blk.tensor
                        def _(te):
                            te.wait_ge(sCE, 2)
                            for h in range(NH):
                                te.wait_ge(sH2, ch * NH + h + 1)
                                te.matmul(psH[:, (2 * h) * 512:(2 * h + 1) * 512],
                                          ksc[:, 0:128], vcs[:, h * 512:(h + 1) * 512],
                                          start=False, stop=False)
                                te.matmul(psH[:, (2 * h + 1) * 512:(2 * h + 2) * 512],
                                          ksc[:, 128:256], vcs[:, h * 512:(h + 1) * 512],
                                          start=False, stop=False).then_inc(sH3, 1)

                with nc.Block() as blk:
                    @blk.tensor
                    def _(te):
                        te.wait_ge(sH3, NCH * NH)
                        last = None
                        for b in range(8):
                            last = te.matmul(psH[:, b * 512:(b + 1) * 512], zlh[:], zrh[:],
                                             start=False, stop=True)
                        last.then_inc(sCE, 1)

                    @blk.vector
                    def _(v):
                        v.wait_ge(sCE, 3)
                        v.tensor_copy(shs[:], psH[:]).then_inc(sCE, 1)

                    @blk.gpsimd
                    def _(g):
                        g.wait_ge(sCE, 4)
                        g.dma_start(out=ccin.ap(), in_=shs[:]).then_inc(sCC, 16)
                        g.wait_ge(sCC, 16)
                        g.collective_compute(
                            "AllGather", ALU.bypass,
                            replica_groups=[[0, 1], [2, 3], [4, 5], [6, 7]],
                            ins=[ccin.ap().opt()], outs=[ccout.ap().opt()],
                        ).then_inc(p.sem(tag + "CCE"), 1)
                        g.wait_ge(p.sem(tag + "CCE"), 1)
                        g.dma_start(out=shs[:], in_=ccout.ap()[0:128, :]).then_inc(sCC, 16)

                    @blk.vector
                    def _(v):
                        v.wait_ge(sCC, 32)
                        v.tensor_scalar(S_sb[:], shs[:], par[:, 0:1], None, ALU.mult)
                        v.drain()
                        v.tensor_copy(S_bf[:], S_sb[:]).then_inc(sCE, 1)

            if CSTOP == "c2":
                return (sCE, 5)
            tA = p.sem(tag + "tA")
            tB = p.sem(tag + "tB")
            tC = p.sem(tag + "tC")
            vA = p.sem(tag + "vA")
            vK = p.sem(tag + "vK")
            vE = p.sem(tag + "vE")
            sOD = p.sem(tag + "OD")
            sOS = p.sem(tag + "OS")
            with contextlib.ExitStack() as e4:
                E4 = e4.enter_context
                qcs = E4(nc.sbuf_tensor(tag + "_sb_qcs", [128, 8 * 64], BF16))
                kcs = E4(nc.sbuf_tensor(tag + "_sb_kcs", [128, 8 * 64], BF16))
                ktcs2 = E4(nc.sbuf_tensor(tag + "_sb_ktcs2", [64, 1024], BF16))
                vcs2 = E4(nc.sbuf_tensor(tag + "_sb_vcs2", [64, 2048], BF16))
                ksc2 = E4(nc.sbuf_tensor(tag + "_sb_ksc2", [64, 256], BF16))
                absb = E4(nc.sbuf_tensor(tag + "_sb_absb", [64, 64], BF16))
                t1 = E4(nc.sbuf_tensor(tag + "_sb_t1", [64, 512], F32))
                osb = E4(nc.sbuf_tensor(tag + "_sb_osb", [64, 512], BF16))
                psST = E4(nc.psum_tensor(tag + "_psST", [64, 64], F32))
                psA = E4(nc.psum_tensor(tag + "_psA", [64, 512], F32))
                psB = E4(nc.psum_tensor(tag + "_psB", [64, 512], F32))
                psD = E4(nc.psum_tensor(tag + "_psD", [128, 2 * 512], F32))

                with p.fori(0, NCH) as ch:
                    with nc.Block() as blk:
                        @blk.sync
                        def _(sy):
                            sy.wait_ge(tB, ch * NH)
                            sy.wait_ge(tC, ch * NH)
                            sy.dma_start(out=qcs[:].rearrange("p (o c) -> p o c", o=8),
                                         in_=qd.rearrange("p (o t) -> p o t", o=8)[:, :, ts(ch, 64)]
                                         ).then_inc(sOD, 16)
                            sy.dma_start(out=kcs[:].rearrange("p (o c) -> p o c", o=8),
                                         in_=kd.rearrange("p (o t) -> p o t", o=8)[:, :, ts(ch, 64)]
                                         ).then_inc(sOD, 16)
                            sy.dma_start(out=ktcs2[:], in_=ktd[ts(ch, 64), :]).then_inc(sOD, 16)
                            sy.dma_start(out=vcs2[:], in_=vd[ts(ch, 64), :]).then_inc(sOD, 16)
                            for h in range(NH):
                                sy.wait_ge(vE, ch * NH + h + 1)
                                sy.dma_start(out=odv[h][ts(ch, 64), :],
                                             in_=osb[:]).then_inc(sOS, 16)

                        @blk.vector
                        def _(v):
                            for h in range(NH):
                                v.wait_ge(sOD, (ch + 1) * 64)
                                v.wait_ge(tC, ch * NH + h)
                                v.tensor_scalar(ksc2[:], ktcs2[:, h * 256:(h + 1) * 256],
                                                gkc[:, h:h + 1], None, ALU.mult).then_inc(vK, 1)
                                v.wait_ge(tA, ch * NH + h + 1)
                                v.wait_ge(tB, ch * NH + h)
                                v.tensor_tensor(absb[:], psST[:],
                                                dmT[:, h * 64:(h + 1) * 64],
                                                ALU.mult).then_inc(vA, 1)
                                v.wait_ge(tB, ch * NH + h + 1)
                                v.wait_ge(tC, ch * NH + h + 1)
                                v.wait_ge(sOS, (ch * NH + h) * 16)
                                v.tensor_scalar(t1[:], psB[:], ginc[:, h:h + 1], None, ALU.mult)
                                v.drain()
                                v.tensor_tensor(osb[:], t1[:], psA[:], ALU.add)
                                for dt_ in range(2):
                                    v.scalar_tensor_tensor(
                                        S_sb[:, (h * 2 + dt_) * 512:(h * 2 + dt_ + 1) * 512],
                                        S_sb[:, (h * 2 + dt_) * 512:(h * 2 + dt_ + 1) * 512],
                                        gCc[:, h:h + 1],
                                        psD[:, dt_ * 512:(dt_ + 1) * 512],
                                        ALU.mult, ALU.add)
                                v.drain()
                                v.tensor_copy(S_bf[:, (h * 2) * 512:(h * 2 + 2) * 512],
                                              S_sb[:, (h * 2) * 512:(h * 2 + 2) * 512])
                                v.drain()
                                v.nop().then_inc(vE, 1)

                        @blk.tensor
                        def _(te):
                            te.wait_ge(sCE, 5)
                            for h in range(NH):
                                te.wait_ge(sOD, (ch + 1) * 64)
                                te.wait_ge(vA, ch * NH + h)
                                te.matmul(psST[:], kcs[:, (2 * h) * 64:(2 * h + 1) * 64],
                                          qcs[:, (2 * h) * 64:(2 * h + 1) * 64],
                                          start=True, stop=False)
                                te.matmul(psST[:], kcs[:, (2 * h + 1) * 64:(2 * h + 2) * 64],
                                          qcs[:, (2 * h + 1) * 64:(2 * h + 2) * 64],
                                          start=False, stop=True).then_inc(tA, 1)
                                te.wait_ge(vA, ch * NH + h + 1)
                                te.wait_ge(vE, ch * NH + h)
                                te.matmul(psA[:], absb[:], vcs2[:, h * 512:(h + 1) * 512],
                                          start=True, stop=True)
                                te.matmul(psB[:], qcs[:, (2 * h) * 64:(2 * h + 1) * 64],
                                          S_bf[:, (2 * h) * 512:(2 * h + 1) * 512],
                                          start=True, stop=False)
                                te.matmul(psB[:], qcs[:, (2 * h + 1) * 64:(2 * h + 2) * 64],
                                          S_bf[:, (2 * h + 1) * 512:(2 * h + 2) * 512],
                                          start=False, stop=True).then_inc(tB, 1)
                                te.wait_ge(vK, ch * NH + h + 1)
                                te.matmul(psD[:, 0:512], ksc2[:, 0:128],
                                          vcs2[:, h * 512:(h + 1) * 512],
                                          start=True, stop=True)
                                te.matmul(psD[:, 512:1024], ksc2[:, 128:256],
                                          vcs2[:, h * 512:(h + 1) * 512],
                                          start=True, stop=True).then_inc(tC, 1)

        if CSTOP == "c4":
            return (p.sem(tag + "OS"), NCH * NH * 16)
        prev_done = None
        for qrt in range(4):
            sQ5 = p.sem(tag + f"Q5{qrt}")
            sT5 = p.sem(tag + f"T5{qrt}")
            sC5 = p.sem(tag + f"C5{qrt}")
            sD5 = p.sem(tag + f"D5{qrt}")
            with contextlib.ExitStack() as e5:
                E5 = e5.enter_context
                ogfm = E5(nc.sbuf_tensor(f"{tag}_sb_ogfm{qrt}", [128, FT * 1024], BF16))
                osb4 = E5(nc.sbuf_tensor(f"{tag}_sb_osb4{qrt}", [128, 2048], BF16))
                gsb4 = E5(nc.sbuf_tensor(f"{tag}_sb_gsb4{qrt}", [128, 2048], BF16))
                scr5 = E5(nc.sbuf_tensor(f"{tag}_sb_scr5{qrt}", [128, 512], F32))
                ss5 = E5(nc.sbuf_tensor(f"{tag}_sb_ss5{qrt}", [128, 8], F32))
                ogt = E5(nc.sbuf_tensor(f"{tag}_sb_ogt{qrt}", [128, 2048], BF16))
                gnwbc = E5(nc.sbuf_tensor(f"{tag}_sb_gnwbc{qrt}", [128, 2048], BF16))
                psG5 = E5(nc.psum_tensor(f"{tag}_psG5{qrt}", [128, 4 * 512], F32))
                psT5 = E5(nc.psum_tensor(f"{tag}_psT5{qrt}", [128, 2 * 1024], BF16))

                entry = (vE, NCH * NH) if prev_done is None else prev_done

                with nc.Block() as blk:
                    @blk.tensor
                    def _(te, entry=entry):
                        te.wait_ge(entry[0], entry[1])
                        te.wait_ge(sOS, NCH * NH * 16)
                        last = None
                        for b in range(4):
                            last = te.matmul(psG5[:, b * 512:(b + 1) * 512], p.onerow_f[0:1, :],
                                             gnw[:, b * 512:(b + 1) * 512],
                                             start=True, stop=True)
                        last.then_inc(sQ5, 1)

                    @blk.vector
                    def _(v, entry=entry):
                        v.wait_ge(sQ5, 1)
                        v.wait_ge(entry[0], entry[1])
                        v.tensor_copy(gnwbc[:], psG5[:]).then_inc(sQ5, 1)

                for t5 in range(8):
                    tt = qrt * 8 + t5
                    base = 2 + t5 * 2
                    with nc.Block() as blk:
                        @blk.sync
                        def _(sy, tt=tt, t5=t5, base=base, entry=entry):
                            sy.wait_ge(sQ5, base if t5 > 0 else 2)
                            if t5 == 0:
                                sy.wait_ge(entry[0], entry[1])
                                sy.wait_ge(sOS, NCH * NH * 16)
                            for h in range(NH):
                                sy.dma_start(out=osb4[:, h * 512:(h + 1) * 512],
                                             in_=odv[h][ts(tt, 128), :]).then_inc(sD5, 16)
                            sy.dma_start(out=gsb4[:], in_=gd[ts(tt, 128), :]).then_inc(sD5, 16)

                        @blk.scalar
                        def _(sc, t5=t5, base=base):
                            sc.wait_ge(sD5, (t5 + 1) * 80)
                            sc.wait_ge(sQ5, base if t5 > 0 else 2)
                            for h in range(NH):
                                sc.activation(scr5[:], osb4[:, h * 512:(h + 1) * 512],
                                              AF.Square, accum_out=ss5[:, h:h + 1])
                            sc.activation(ss5[:, 4:8], ss5[:, 0:4], AF.Ln,
                                          scale=cst[:, 6:7], bias=cst[:, 0:1])
                            sc.activation(ss5[:, 4:8], ss5[:, 4:8], AF.Exp,
                                          scale=cst[:, 1:2]).then_inc(sQ5, 1)

                        @blk.vector
                        def _(v, t5=t5, base=base):
                            v.wait_ge(sQ5, base + 1)
                            v.wait_ge(sC5, t5 * 16)
                            for h in range(NH):
                                v.tensor_scalar(osb4[:, h * 512:(h + 1) * 512],
                                                osb4[:, h * 512:(h + 1) * 512],
                                                ss5[:, 4 + h:5 + h], None, ALU.mult)
                            v.drain()
                            v.tensor_tensor(osb4[:], osb4[:], gnwbc[:], ALU.mult)
                            v.drain()
                            v.tensor_tensor(ogt[:], osb4[:], gsb4[:],
                                            ALU.mult).then_inc(sQ5, 1)
                            for ft in range(FT):
                                pp = ft % 2
                                v.wait_ge(sT5, t5 * 16 + ft + 1)
                                v.tensor_copy(ogfm[:, ft * 1024 + t5 * 128: ft * 1024 + (t5 + 1) * 128],
                                              psT5[:, pp * 1024:pp * 1024 + 128]).then_inc(sC5, 1)

                        @blk.tensor
                        def _(te, t5=t5, base=base):
                            te.wait_ge(sQ5, base + 2)
                            for ft in range(FT):
                                pp = ft % 2
                                te.wait_ge(sC5, t5 * 16 + max(0, ft - 1))
                                te.transpose(psT5[:, pp * 1024:pp * 1024 + 128],
                                             ogt[:, ft * 128:(ft + 1) * 128],
                                             p.identb[:]).then_inc(sT5, 1)

                if CSTOP == "c5tiles":
                    return (p.sem(tag + f"C5{qrt}"), 8 * 16)
                sW6 = p.sem(tag + f"W6{qrt}")
                sX6 = p.sem(tag + f"X6{qrt}")
                sM6 = p.sem(tag + f"M6{qrt}")
                sE6 = p.sem(tag + f"E6{qrt}")
                sS6 = p.sem(tag + f"S6{qrt}")
                x0q = E5(nc.sbuf_tensor(f"{tag}_sb_x0q{qrt}", [128, 1024], F32))
                x1q = E5(nc.sbuf_tensor(f"{tag}_sb_x1q{qrt}", [128, 1024], F32))
                wos = E5(nc.sbuf_tensor(f"{tag}_sb_wos{qrt}", [128, FT * 128], BF16))
                psW6 = E5(nc.psum_tensor(f"{tag}_psW6{qrt}", [128, 2 * 512], F32))
                with p.fori(0, FT) as o:
                    with nc.Block() as blk:
                        @blk.gpsimd
                        def _(g, qrt=qrt):
                            g.wait_ge(sC5, 8 * 16)
                            g.wait_ge(sM6, o)
                            g.dma_start(out=wos[:].rearrange("p (i m) -> p i m", i=FT),
                                        in_=w["wo_strips"][o].rearrange("i p m -> p i m")
                                        ).then_inc(sW6, 16)

                        @blk.sync
                        def _(sy, qrt=qrt):
                            sy.wait_ge(sS6, o * 16)
                            sy.dma_start(out=x0q[:],
                                         in_=x_in_dram[ts(o, 128), qrt * 1024:(qrt + 1) * 1024]
                                         ).then_inc(sX6, 16)
                            sy.wait_ge(sE6, o + 1)
                            sy.dma_start(out=x_out_dram[ts(o, 128), qrt * 1024:(qrt + 1) * 1024],
                                         in_=x1q[:]).then_inc(sS6, 16)

                        @blk.tensor
                        def _(te, qrt=qrt):
                            te.wait_ge(sW6, (o + 1) * 16)
                            te.wait_ge(sE6, o)
                            last = None
                            for i in range(FT):
                                for q2 in range(2):
                                    last = te.matmul(psW6[:, q2 * 512:(q2 + 1) * 512],
                                                     wos[:, i * 128:(i + 1) * 128],
                                                     ogfm[:, i * 1024 + q2 * 512: i * 1024 + (q2 + 1) * 512],
                                                     start=(i == 0), stop=(i == FT - 1))
                            last.then_inc(sM6, 1)

                        @blk.vector
                        def _(v, qrt=qrt):
                            v.wait_ge(sM6, o + 1)
                            v.wait_ge(sX6, (o + 1) * 16)
                            v.wait_ge(sS6, o * 16)
                            v.tensor_tensor(x1q[:, 0:512], psW6[:, 0:512], x0q[:, 0:512], ALU.add)
                            v.tensor_tensor(x1q[:, 512:1024], psW6[:, 512:1024],
                                            x0q[:, 512:1024], ALU.add).then_inc(sE6, 1)

                prev_done = (p.sem(tag + f"S6{qrt}"), FT * 16)
                if CSTOP == "c6q0":
                    return prev_done
    return prev_done



def build_program(blocks=("A",)):
    nc = bacc.Bacc("TRN2", target_bir_lowering=False,
                   detect_race_conditions=not globals().get("NO_RACE_DETECT", False))
    p = P(nc)

    x0 = nc.dram_tensor("x0", [D, T], F32, kind="ExternalInput").ap()
    ident_in = nc.dram_tensor("ident", [128, 128], F32, kind="ExternalInput").ap()
    wmaps = {}
    if "A" in blocks:
        wmaps["A"] = {
            "wq_strips": nc.dram_tensor("A_wq", [16, FT, 128, 128], F32, kind="ExternalInput").ap(),
            "k_fm": nc.dram_tensor("A_kfm", [16, 128, T], F32, kind="ExternalInput").ap(),
            "v_tiles": nc.dram_tensor("A_vt", [16, 32, 128, 128], F32, kind="ExternalInput").ap(),
            "wo_strips": nc.dram_tensor("A_wo", [FT, FT, 128, 128], F32, kind="ExternalInput").ap(),
        }
    if "B" in blocks:
        wmaps["B"] = {
            "rt8": nc.dram_tensor("B_rt", [FT, 128, 8], F32, kind="ExternalInput").ap(),
            "identf": ident_in,
            "sel": nc.dram_tensor("B_sel", [128, 16], F32, kind="ExternalInput").ap(),
            "msel": nc.dram_tensor("B_msel", [16, 1024], F32, kind="ExternalInput").ap(),
            "bg": nc.dram_tensor("B_wg", [88, FT, 128, 128], F32, kind="ExternalInput").ap(),
            "bu": nc.dram_tensor("B_wu", [88, FT, 128, 128], F32, kind="ExternalInput").ap(),
            "bd": nc.dram_tensor("B_wd", [8, FT, 11, 128, 128], F32, kind="ExternalInput").ap(),
        }
    if "C" in blocks:
        wmaps["C"] = {
            "wq_strips": nc.dram_tensor("C_wq", [8, FT, 128, 128], F32, kind="ExternalInput").ap(),
            "wk_strips": nc.dram_tensor("C_wk", [8, FT, 128, 128], F32, kind="ExternalInput").ap(),
            "wvT": nc.dram_tensor("C_wv", [FT, 128, 2048], F32, kind="ExternalInput").ap(),
            "wgT": nc.dram_tensor("C_wg", [FT, 128, 2048], F32, kind="ExternalInput").ap(),
            "wo_strips": nc.dram_tensor("C_wo", [FT, FT, 128, 128], F32, kind="ExternalInput").ap(),
            "ropecos": nc.dram_tensor("C_rc", [128, T], F32, kind="ExternalInput").ap(),
            "ropesin": nc.dram_tensor("C_rs", [128, T], F32, kind="ExternalInput").ap(),
            "gk": nc.dram_tensor("C_gk", [64, 4], F32, kind="ExternalInput").ap(),
            "gin": nc.dram_tensor("C_gin", [64, 4], F32, kind="ExternalInput").ap(),
            "grev": nc.dram_tensor("C_grev", [64, 256], F32, kind="ExternalInput").ap(),
            "dmT": nc.dram_tensor("C_dmT", [64, 256], F32, kind="ExternalInput").ap(),
            "gC": nc.dram_tensor("C_gC", [128, 4], F32, kind="ExternalInput").ap(),
            "parity": nc.dram_tensor("C_par", [128, 1], F32, kind="ExternalInput").ap(),
            "gnw": nc.dram_tensor("C_gnw", [1, 2048], F32, kind="ExternalInput").ap(),
        }
    if "D" in blocks:
        wmaps["D"] = {
            "amp": nc.dram_tensor("D_amp", [128, 16], F32, kind="ExternalInput").ap(),
            "escal_row": nc.dram_tensor("D_esr", [1, 128], F32, kind="ExternalInput").ap(),
            "icol": nc.dram_tensor("D_icol", [128, 128], F32, kind="ExternalInput").ap(),
            "router": nc.dram_tensor("D_rt", [FT, 128, 128], F32, kind="ExternalInput").ap(),
            "g_strips": nc.dram_tensor("D_wg", [64, FT, 128, 128], F32, kind="ExternalInput").ap(),
            "u_strips": nc.dram_tensor("D_wu", [64, FT, 128, 128], F32, kind="ExternalInput").ap(),
            "d_strips": nc.dram_tensor("D_wd", [16, 8, 128, 128], F32, kind="ExternalInput").ap(),
            "shg_strips": nc.dram_tensor("D_sg", [16, FT, 128, 128], F32, kind="ExternalInput").ap(),
            "shu_strips": nc.dram_tensor("D_su", [16, FT, 128, 128], F32, kind="ExternalInput").ap(),
            "shd_strips": nc.dram_tensor("D_sd", [16, FT, 128, 128], F32, kind="ExternalInput").ap(),
            "rep": nc.dram_tensor("D_rep", [8, 1024], F32, kind="ExternalInput").ap(),
        }

    xs = [x0]
    for bi in range(len(blocks) - 1):
        xs.append(nc.dram_tensor(f"xi{bi}", [D, T], F32, kind="Internal").ap())
    xs.append(nc.dram_tensor("xout", [D, T], F32, kind="ExternalOutput").ap())
    if globals().get("DDBG"):
        p.dbg_cem = nc.dram_tensor("dbg_cem", [128, T], F32, kind="ExternalOutput").ap()
        p.dbg_mxr = nc.dram_tensor("dbg_mxr", [128, 1024], F32, kind="ExternalOutput").ap()
    if globals().get("BDBG"):
        p.dbg_bemI = nc.dram_tensor("dbg_bemI", [16, T], F32, kind="ExternalOutput").ap()
        p.dbg_bemW = nc.dram_tensor("dbg_bemW", [16, T], F32, kind="ExternalOutput").ap()
        p.dbg_sh = nc.dram_tensor("dbg_sh", [128, FT * 1024], F32, kind="ExternalOutput").ap()

    with contextlib.ExitStack() as es:
        E = es.enter_context
        dram_base = nc.local_dram_base
        shared_base = nc.shared_dram_base

        gate = None
        for bi, b in enumerate(blocks):
            if bi > 0:
                with nc.Block() as blk:
                    @blk.sync
                    def _(sy, gate=gate):
                        sy.wait_ge(gate[0], gate[1])
                nc.reset(previous_local_dram_base=dram_base,
                         previous_shared_dram_base=shared_base)
                p._sems.clear()
                gate = None
            x_in, x_out = xs[bi], xs[bi + 1]
            with contextlib.ExitStack() as es2:
                E2 = es2.enter_context
                cst = E2(nc.sbuf_tensor(f"cst{bi}", [128, 8], F32))
                onecol = E2(nc.sbuf_tensor(f"onecol{bi}", [128, 1], BF16))
                onerow = E2(nc.sbuf_tensor(f"onerow{bi}", [1, 128], BF16))
                p.onerow_f = E2(nc.sbuf_tensor(f"onerow_f{bi}", [1, 128], F32))
                identb = E2(nc.sbuf_tensor(f"identb{bi}", [128, 128], BF16))
                p.identb = identb
                emit_consts(p, cst, onecol, onerow, identb, ident_in)
                xbf = E2(nc.sbuf_tensor(f"xbf{bi}", [128, FT * T], BF16))
                if b == "A":
                    sN = emit_stage_norm(p, f"S{bi}A", x_in, xbf, cst, identb,
                                         onecol, onerow, gate=gate)
                    sS = emit_block_a(p, f"B{bi}A", xbf, sN, x_in, x_out,
                                      wmaps["A"], cst, onecol, onerow)
                    gate = (sS, 16 * FT)
                elif b == "B":
                    gate = emit_block_b(p, f"L{bi}B", xbf, x_in, x_out,
                                        wmaps["B"], cst, onecol, onerow, gate=gate)
                elif b == "C":
                    gate = emit_block_c(p, f"B{bi}C", xbf, x_in, x_out,
                                        wmaps["C"], cst, onecol, onerow, gate=gate)
                elif b == "D":
                    gate = emit_block_d(p, f"B{bi}D", xbf, x_in, x_out,
                                        wmaps["D"], cst, onecol, onerow, gate=gate)
                else:
                    raise ValueError(b)

        with nc.Block() as blk:
            @blk.sync
            def _(sy):
                sy.wait_ge(gate[0], gate[1])
    return nc


def _prep_shared(inputs, blocks):
    sh = {"ident": np.eye(128, dtype=np.float32)}
    if "A" in blocks:
        wq = np.asarray(inputs["L0_attn_Wq"], np.float32)
        wo = np.asarray(inputs["L0_attn_Wo"], np.float32)
        kv_k = np.asarray(inputs["L0_attn_kv_k"], np.float32).reshape(4096, 16, 128)
        kv_v = np.asarray(inputs["L0_attn_kv_v"], np.float32).reshape(32, 128, 16, 128)
        sh.update({
            "A_wq": _strips(np.ascontiguousarray(wq.T)),
            "A_kfm": np.ascontiguousarray(kv_k.transpose(1, 2, 0)),
            "A_vt": np.ascontiguousarray(kv_v.transpose(2, 0, 1, 3)),
            "A_wo": _strips(np.ascontiguousarray(wo.T)),
        })
    if "B" in blocks:
        wr = np.asarray(inputs["L0_ffn_Wr"], np.float32)
        wgB = np.asarray(inputs["L0_ffn_Wg"], np.float32)
        wuB = np.asarray(inputs["L0_ffn_Wu"], np.float32)
        wdB = np.asarray(inputs["L0_ffn_Wd"], np.float32)
        msel = np.zeros((16, 1024), np.float32)
        for e in range(8):
            msel[e, e * 128:(e + 1) * 128] = 1.0
            msel[8 + e, e * 128:(e + 1) * 128] = 1.0
        sh.update({
            "B_rt": np.ascontiguousarray(wr.T).reshape(FT, 128, 8),
            "B_wg": np.concatenate(
                [_strips(np.ascontiguousarray(wgB[e].T)) for e in range(8)], 0),
            "B_wu": np.concatenate(
                [_strips(np.ascontiguousarray(wuB[e].T)) for e in range(8)], 0),
            "B_wd": np.stack(
                [_strips(np.ascontiguousarray(wdB[e].T)) for e in range(8)], 0),
            "B_msel": msel,
        })
    if "C" in blocks:
        wq = np.asarray(inputs["L1_attn_Wq"], np.float32)
        wk = np.asarray(inputs["L1_attn_Wk"], np.float32)
        wv = np.asarray(inputs["L1_attn_Wv"], np.float32)
        wg = np.asarray(inputs["L1_attn_Wg"], np.float32)
        wo = np.asarray(inputs["L1_attn_Wo"], np.float32)
        gamma = (1.0 - 2.0 ** (-5.0 - np.arange(4))).astype(np.float64)
        j = np.arange(64)
        gk = np.stack([g ** (63 - j) for g in gamma], 1).astype(np.float32)
        gin = np.stack([g ** (j + 1) for g in gamma], 1).astype(np.float32)
        grev = np.zeros((64, 256), np.float32)
        dmT = np.zeros((64, 256), np.float32)
        for h, g in enumerate(gamma):
            for tau in range(64):
                with np.errstate(under="ignore"):
                    grev[:, h * 64 + tau] = g ** (4095.0 - (64.0 * tau + j))
            e = j[None, :] - j[:, None]
            with np.errstate(under="ignore"):
                dmT[:, h * 64:(h + 1) * 64] = np.where(e >= 0, g ** np.maximum(e, 0), 0.0)
        gC = np.broadcast_to((gamma ** 64).astype(np.float32), (128, 4)).copy()
        inv_freq = (1.0 / 10000.0 ** (np.arange(0, 256, 2) / 256.0))
        sh.update({
            "C_wq": _strips(np.ascontiguousarray(wq.T)),
            "C_wk": _strips(np.ascontiguousarray(wk.T)),
            "C_wv": np.ascontiguousarray(np.ascontiguousarray(wv.T).reshape(FT, 128, 2048)),
            "C_wg": np.ascontiguousarray(np.ascontiguousarray(wg.T).reshape(FT, 128, 2048)),
            "C_wo": _strips(np.ascontiguousarray(wo.T)),
            "C_gk": gk, "C_gin": gin, "C_grev": grev, "C_dmT": dmT, "C_gC": gC,
            "C_gnw": np.ascontiguousarray(
                np.tile(np.asarray(inputs["L1_attn_gnorm_w"], np.float32), 4)[None, :]),
        })
        sh["_C_invfreq"] = inv_freq
    if "D" in blocks:
        sh.update({
            "D_amp": np.ascontiguousarray(
                np.asarray(inputs["L1_ffn_amplitudes"], np.float32).reshape(128, 16)),
            "D_esr": np.ascontiguousarray(
                np.asarray(inputs["L1_ffn_expert_scale"], np.float32).reshape(1, 128)),
            "D_icol": np.broadcast_to(np.arange(128, dtype=np.float32), (128, 128)).copy(),
            "D_rt": np.ascontiguousarray(
                np.asarray(inputs["L1_ffn_w_router"], np.float32).reshape(FT, 128, 128)),
            "D_wg": _strips(np.asarray(inputs["L1_ffn_w_gate"], np.float32)),
            "D_wu": _strips(np.asarray(inputs["L1_ffn_w_up"], np.float32)),
            "D_wd": _strips(np.asarray(inputs["L1_ffn_w_down"], np.float32)),
            "D_sg": _strips(np.asarray(inputs["L1_ffn_shared_w_gate"], np.float32)),
            "D_su": _strips(np.asarray(inputs["L1_ffn_shared_w_up"], np.float32)),
            "D_sd": _strips(np.asarray(inputs["L1_ffn_shared_w_down"], np.float32)),
            "D_rep": np.eye(8, dtype=np.float32).repeat(128, axis=1) * 0 + np.kron(
                np.eye(8, dtype=np.float32), np.ones((1, 128), np.float32)),
        })
    return sh


BLOCKS = ("A", "B", "C", "D")


def kernel(**inputs):
    blocks = BLOCKS
    shared = _prep_shared(inputs, blocks)
    x = np.asarray(inputs["L0_attn_query"], np.float32).reshape(32768, D)
    in_maps = []
    invf = shared.pop("_C_invfreq", None)
    for c in range(8):
        m = {"x0": np.ascontiguousarray(x[c * T:(c + 1) * T].T)}
        m.update(shared)
        if invf is not None:
            pos = (c % 2) * T + np.arange(T)
            fr = pos[None, :] * invf[:, None]
            m["C_rc"] = np.cos(fr).astype(np.float32)
            m["C_rs"] = np.sin(fr).astype(np.float32)
            m["C_par"] = np.full((128, 1), float(c % 2), np.float32)
        if "B" in blocks:
            sel = np.zeros((128, 16), np.float32)
            for cp in range(8):
                for e in range(8):
                    if cp < c:
                        sel[cp * 16 + e, e] = 1.0
                        sel[cp * 16 + 8 + e, 8 + e] = 1.0
                    sel[cp * 16 + e, 8 + e] = 1.0
            m["B_sel"] = sel
        in_maps.append(m)
    nc = build_program(blocks)
    nc.finalize()
    res = run_bass_kernel_spmd(nc, in_maps, core_ids=list(range(8)))
    globals()["LAST_RES"] = res
    out = np.empty((32768, D), np.float32)
    for c in range(8):
        out[c * T:(c + 1) * T] = res.results[c]["xout"].T
    return out.reshape(4, 8192, D)
